# version 27
# baseline (speedup 1.0000x reference)
import numpy as np
import ml_dtypes
import concourse.bass as bass
import concourse.mybir as mybir
from concourse.bass_utils import run_bass_kernel_spmd

F32 = mybir.dt.float32
BF16 = mybir.dt.bfloat16
I32 = mybir.dt.int32
AF = mybir.ActivationFunctionType
ALU = mybir.AluOpType
AX = mybir.AxisListType
bf16 = ml_dtypes.bfloat16

D = 1024
T = 2048
NH = 16
DH = 64
NC = 8
NT = 16
PI = float(np.pi)
EXP_C = 30.0

ENGS = ("pe", "act", "dve", "pool", "sync")
OWN = {"pe": "pe", "act": "act", "dve": "dve", "pool": "pool"}


class Sched:

    def __init__(self, nc):
        self.nc = nc
        self.q = {e: [] for e in ENGS}
        names = ["pe", "act", "dve", "pool", "dmas", "dmag", "out",
                 "x0", "x1", "x2", "x3"]
        self.sem = {k: nc.alloc_semaphore("s_" + k) for k in names}
        self.cnt = {k: 0 for k in names}
        self.watermark = {e: {k: 0 for k in names} for e in ENGS}

    def emit(self, eng, fn, waits=(), inc=None, amt=1):
        ws = []
        for s, v in waits:
            if v is None:
                continue
            assert v <= self.cnt[s], f"future token {s}={v} (now {self.cnt[s]})"
            if v > self.watermark[eng][s]:
                self.watermark[eng][s] = v
                ws.append((s, v))
        tok = None
        if inc is not None:
            self.cnt[inc] += amt
            tok = (inc, self.cnt[inc])
            if OWN.get(eng) == inc:
                self.watermark[eng][inc] = self.cnt[inc]
        sem = self.sem

        def run(e, fn=fn, ws=ws, inc=inc, amt=amt):
            for s, v in ws:
                e.wait_ge(sem[s], v)
            inst = fn(e)
            if inc is not None:
                inst.then_inc(sem[inc], amt)

        self.q[eng].append(run)
        return tok

    def tok(self, s):
        return (s, self.cnt[s])


def build(upto="full", taps=None):
    nc = bass.Bass("TRN2", target_bir_lowering=False)

    x = nc.dram_tensor("x", [T, D], F32, kind="ExternalInput")
    wqkv = nc.dram_tensor("wqkv", [D, 3 * D], F32, kind="ExternalInput")
    wo_d = nc.dram_tensor("wo", [D, D], F32, kind="ExternalInput")
    wft = nc.dram_tensor("wft", [D, D], F32, kind="ExternalInput")
    rsw_d = nc.dram_tensor("rsw", [128, 128], BF16, kind="ExternalInput")
    y = nc.dram_tensor("y", [T, D], F32, kind="ExternalOutput")

    xnT = nc.alloc_sbuf_tensor("xnT", [128, NC * T], BF16)
    qT = nc.alloc_sbuf_tensor("qT", [128, NC * T], BF16)
    kT = nc.alloc_sbuf_tensor("kT", [128, NC * T], BF16)
    vA = nc.alloc_sbuf_tensor("vA", [128, NT * NH * 65 + 64], BF16)
    wo8 = nc.alloc_sbuf_tensor("wo8", [128, 8192], BF16)
    swT = nc.alloc_sbuf_tensor("swT", [128, 8192], BF16)
    wd = nc.alloc_sbuf_tensor("wd", [128, 2048], BF16)
    qb = nc.alloc_sbuf_tensor("qb", [128, 1024], BF16)
    pt = nc.alloc_sbuf_tensor("pt", [128, 3072], BF16)
    rrow = nc.alloc_sbuf_tensor("rrow", [128, 2048], BF16)
    rrow32 = nc.alloc_sbuf_tensor("rrow32", [128, 512], F32)
    bcSB = nc.alloc_sbuf_tensor("bcSB", [64, 1024], BF16)
    ytt = nc.alloc_sbuf_tensor("ytt", [64, 1024], BF16)
    pool4 = nc.alloc_sbuf_tensor("pool4", [128, 4096], F32)
    poolb = nc.alloc_sbuf_tensor("poolb", [128, 4096], BF16)
    identf = nc.alloc_sbuf_tensor("identf", [128, 128], F32)
    rsw = nc.alloc_sbuf_tensor("rswb", [128, 128], BF16)
    onesb = nc.alloc_sbuf_tensor("onesb", [128, 64], BF16)
    biasC = nc.alloc_sbuf_tensor("biasC", [128, 4], F32)
    stats = nc.alloc_sbuf_tensor("stats", [128, 12], F32)
    mv = nc.alloc_sbuf_tensor("mv", [128, 2], F32)
    lncol = nc.alloc_sbuf_tensor("lncol", [128, 48], F32)
    ecol = nc.alloc_sbuf_tensor("ecol", [128, 1], F32)
    trow = swT.ap()[:, 0:4096].bitcast(F32)

    C4 = wo8[:, 0:2048]
    S4 = wo8[:, 2048:4096]

    def p4(i):
        return pool4[:, 1024 * i:1024 * (i + 1)]

    PSA = nc.alloc_psum_tensor("PSA", [128, 2048], F32)
    PSS = nc.alloc_psum_tensor("PSS", [128, 2048], F32)

    S = Sched(nc)
    env = {"xnT": xnT, "qT": qT, "kT": kT, "vA": vA, "wo8": wo8, "swT": swT,
           "pt": pt, "rrow": rrow, "pool4": pool4, "poolb": poolb, "trow": trow,
           "stats": stats, "mv": mv, "lncol": lncol, "qb": qb, "bcSB": bcSB,
           "ecol": ecol, "identf": identf, "onesb": onesb, "biasC": biasC}

    rsw_tok = S.emit("sync", lambda e: e.dma_start(out=rsw[:], in_=rsw_d.ap()),
                     inc="dmas", amt=16)

    S.emit("pool", lambda e: e.memset(identf[:], 1.0))
    id_tok = S.emit("pool", lambda e: e.affine_select(
        out=identf[:], in_=identf[:], pattern=[[-1, 128]],
        compare_op=ALU.is_equal, fill=0.0, channel_multiplier=1, base=0),
        inc="pool")
    ones_tok = S.emit("pool", lambda e: e.memset(onesb[:], 1.0), inc="pool")
    S.emit("pool", lambda e: e.memset(vA.ap()[:, NT * NH * 65:], 0.0))
    vones_tok = S.emit("pool", lambda e: e.memset(
        vA.ap()[:, 64:NT * NH * 65:65], 1.0), inc="pool")
    bias_tok = S.emit("dve", lambda e: e.memset(biasC[:, 0:1], -EXP_C), inc="dve")
    S.emit("dve", lambda e: e.memset(ecol[:], 1.0))
    S.emit("dve", lambda e: e.memset(ecol[0:32, :], -1.0))
    ecol_tok = S.emit("dve", lambda e: e.memset(ecol[64:96, :], -1.0), inc="dve")



    sqdump = xnT.ap()[:, 0:2048].bitcast(F32)
    red_toks = []
    xslot_tok = {}
    for tau in range(4):
        xslot_tok[tau % 4] = S.emit("sync", lambda e, tau=tau: e.dma_start(
            out=p4(tau), in_=x.ap()[128 * tau:128 * (tau + 1), :]),
            waits=[], inc="x%d" % (tau % 4), amt=16)
    redA = None
    for tau in range(NT):
        par = tau % 4
        xt = p4(par)
        S.emit("act", lambda e, xt=xt, tau=tau: e.activation(
            sqdump, xt, AF.Square, accum_out=lncol[:, 16 + tau:17 + tau]),
            waits=[xslot_tok[par]])
        redA = S.emit("dve", lambda e, xt=xt, tau=tau: e.reduce_sum(
            lncol[:, tau:tau + 1], xt, AX.X), waits=[xslot_tok[par], S.tok("act")],
            inc="dve")
        red_toks.append(redA)
        if tau + 4 < NT:
            xslot_tok[par] = S.emit("sync", lambda e, tau=tau, par=par: e.dma_start(
                out=p4(par), in_=x.ap()[128 * (tau + 4):128 * (tau + 5), :]),
                waits=[redA], inc="x%d" % par, amt=16)
    trow2 = swT.ap()[:, 4096:8192].bitcast(F32)
    it_tok = S.emit("pool", lambda e: e.iota(
        trow[0:1, :], [[1, 2048]], base=0, channel_multiplier=0,
        allow_small_or_imprecise_dtypes=True), inc="pool")
    S.emit("pool", lambda e: e.iota(
        trow2[0:1, 0:128].rearrange("a (r i) -> a r i", i=32),
        [[0, 4], [1, 32]], base=0, channel_multiplier=0,
        allow_small_or_imprecise_dtypes=True), inc="pool")
    S.emit("act", lambda e: e.activation(
        trow2[0:1, 0:128], trow2[0:1, 0:128], AF.Exp,
        scale=-float(np.log(10000.0)) / 32.0), waits=[S.tok("pool")])
    th_tok = S.emit("act", lambda e: e.drain(), inc="act")

    ang_tok = None
    for j in range(2):
        for s2 in range(2):
            cs = slice(1024 * j + 512 * s2, 1024 * j + 512 * s2 + 512)
            ang_tok = S.emit("pe", lambda e, cs=cs: e.matmul(
                PSA[:, cs], trow2[0:1, 0:128], trow[0:1, cs],
                start=True, stop=True), waits=[th_tok, it_tok], inc="pe")

    tgA = poolb.ap()[:, 0:2048].bitcast(F32)
    tgB = poolb.ap()[:, 2048:4096].bitcast(F32)
    sin_last = None
    for (dest, quarter, signed) in [(C4, True, False), (S4, False, True)]:
        for j in range(2):
            cs = slice(1024 * j, 1024 * (j + 1))
            w0 = [ang_tok] + ([sin_last] if sin_last else [])
            S.emit("dve", lambda e, cs=cs, quarter=quarter: e.tensor_scalar(
                tgA, PSA[:, cs], 1.0 / (2 * PI), 0.25 if quarter else 0.0,
                ALU.mult, ALU.add), waits=w0)
            S.emit("dve", lambda e: e.tensor_copy(tgB.bitcast(I32), tgA))
            S.emit("dve", lambda e: e.tensor_copy(tgA, tgB.bitcast(I32)))
            r_tok = S.emit("dve", lambda e, cs=cs: e.scalar_tensor_tensor(
                tgB, tgA, -2 * PI, PSA[:, cs], ALU.mult, ALU.add), inc="dve")
            if quarter:
                r_tok = S.emit("dve", lambda e: e.tensor_scalar_add(
                    tgB, tgB, PI / 2), inc="dve")
            if signed:
                s_tok = S.emit("act", lambda e: e.activation(
                    tgA, tgB, AF.Sin), waits=[r_tok], inc="act")
                sin_last = S.emit("dve", lambda e, dest=dest, cs=cs: e.tensor_scalar_mul(
                    dest[:, cs], tgA, ecol[:, 0:1]), waits=[s_tok, ecol_tok],
                    inc="dve")
            else:
                sin_last = S.emit("act", lambda e, dest=dest, cs=cs: e.activation(
                    dest[:, cs], tgB, AF.Sin), waits=[r_tok], inc="act")
    tables_tok = sin_last
    tables_dve = S.tok("dve")
    tables_act = S.tok("act")
    aA = S.emit("act", lambda e: e.drain(), inc="act")
    S.emit("dve", lambda e: e.drain(), waits=[aA])
    S.emit("dve", lambda e: e.tensor_scalar_mul(
        lncol[:, 0:16], lncol[:, 0:16], -1.0 / D))
    S.emit("dve", lambda e: e.tensor_scalar(
        lncol[:, 16:32], lncol[:, 16:32], 1.0 / D, 1e-5, ALU.mult, ALU.add))
    S.emit("dve", lambda e: e.drain())
    S.emit("dve", lambda e: e.tensor_tensor(
        lncol[:, 32:48], lncol[:, 0:16], lncol[:, 0:16], ALU.mult))
    S.emit("dve", lambda e: e.drain())
    S.emit("dve", lambda e: e.tensor_tensor(
        lncol[:, 16:32], lncol[:, 16:32], lncol[:, 32:48], ALU.subtract))
    vB = S.emit("dve", lambda e: e.drain(), inc="dve")
    S.emit("act", lambda e: e.activation(
        lncol[:, 32:48], lncol[:, 16:32], AF.Sqrt), waits=[vB])
    aB = S.emit("act", lambda e: e.drain(), inc="act")
    S.emit("dve", lambda e: e.reciprocal(lncol[:, 16:32], lncol[:, 32:48]),
           waits=[aB])
    S.emit("dve", lambda e: e.drain())
    S.emit("dve", lambda e: e.tensor_tensor(
        lncol[:, 32:48], lncol[:, 0:16], lncol[:, 16:32], ALU.mult))
    ln_tok = S.emit("dve", lambda e: e.drain(), inc="dve")

    evac_toks = [None, None]
    v_evac = []
    xnbufs = [pt.ap()[:, 0:2048].bitcast(F32),
              rrow.ap()[:, 0:2048].bitcast(F32)]
    wvfull = swT.ap()[:, 0:8192]
    wv_toks = [None, None]
    wv_toks[0] = S.emit("pool", lambda e: e.dma_start(
        out=wvfull.rearrange("p (c n) -> p c n", c=NC),
        in_=wqkv.ap()[:, 2048:3072].rearrange("(c p) n -> p c n", p=128)),
        waits=[tables_tok, tables_dve, tables_act], inc="dmag", amt=16)
    for tau in range(4):
        xslot_tok[tau % 4] = S.emit("sync", lambda e, tau=tau: e.dma_start(
            out=p4(tau), in_=x.ap()[128 * tau:128 * (tau + 1), :]),
            waits=[red_toks[min(tau + 12, NT - 1)]], inc="x%d" % (tau % 4),
            amt=16)
    tp_tok = None
    tp_prev = [None, None]
    vjobs = []
    for tau in range(NT):
        par = tau % 4
        xt = p4(par)
        xn = xnbufs[tau % 2]
        xw = [xslot_tok[par], ln_tok]
        if tp_prev[tau % 2] is not None:
            xw.append(tp_prev[tau % 2])
        xn_tok = S.emit("dve", lambda e, xt=xt, xn=xn, tau=tau: e.tensor_scalar(
            xn, xt, lncol[:, 16 + tau:17 + tau], lncol[:, 32 + tau:33 + tau],
            ALU.mult, ALU.add), waits=xw, inc="dve")
        if tau + 4 < NT:
            xslot_tok[par] = S.emit("sync", lambda e, tau=tau, par=par: e.dma_start(
                out=p4(par), in_=x.ap()[128 * (tau + 4):128 * (tau + 5), :]),
                waits=[xn_tok], inc="x%d" % par, amt=16)
        bpar = tau % 2
        pcols = 1024 * bpar
        for c in range(NC):
            w = [xn_tok, id_tok]
            if c == 0 and evac_toks[bpar] is not None:
                w.append(evac_toks[bpar])
            tp_tok = S.emit("pe", lambda e, xn=xn, c=c, pcols=pcols: e.transpose(
                PSS[:, pcols + 128 * c: pcols + 128 * (c + 1)],
                xn[:, 128 * c:128 * (c + 1)], identf[:]), waits=w, inc="pe")
        tp_prev[tau % 2] = tp_tok

        def mk_out(tau=tau):
            return xnT.ap().rearrange("p (c t) -> p c t", c=NC)[:, :, 128 * tau:128 * (tau + 1)]
        evac_toks[bpar] = S.emit("dve", lambda e, pcols=pcols, mo=mk_out: e.tensor_copy(
            mo(), PSS[:, pcols:pcols + 1024].rearrange("p (c r) -> p c r", c=NC)),
            waits=[tp_tok], inc="dve")

        def vjob(tau=tau, ev=evac_toks[bpar]):
            for vh in range(2):
                pcol = 512 * ((2 * tau + vh) % 4)
                mm_tok = None
                for c in range(NC):
                    w = []
                    if c == 0:
                        w = [S.tok("dmag"), ev]
                        if len(v_evac) >= 4:
                            w.append(v_evac[-4])
                    mm_tok = S.emit("pe", lambda e, c=c, tau=tau, pcol=pcol, vh=vh: e.matmul(
                        PSA[:, pcol:pcol + 512],
                        xnT.ap()[:, T * c + 128 * tau: T * c + 128 * (tau + 1)],
                        wvfull[:, 1024 * c + 512 * vh:1024 * c + 512 * vh + 512],
                        start=(c == 0), stop=(c == NC - 1)), waits=w, inc="pe")

                def mk_vout(tau=tau, vh=vh):
                    a = vA.ap()[:, 0:NT * NH * 65].rearrange("p (t h s) -> p t h s", t=NT, h=NH)
                    return a[:, tau, 8 * vh:8 * vh + 8, 0:64]
                v_evac.append(S.emit("dve", lambda e, pcol=pcol, mo=mk_vout: e.tensor_copy(
                    mo(), PSA[:, pcol:pcol + 512].rearrange("p (h s) -> p h s", h=8)),
                    waits=[mm_tok, vones_tok], inc="dve"))
        vjobs.append(vjob)
        if tau >= 1:
            vjobs.pop(0)()
    while vjobs:
        vjobs.pop(0)()
    xnT_tok = evac_toks[(NT - 1) % 2]
    xnT_oth = evac_toks[NT % 2]

    if upto == "ph1":
        return finish(nc, S, y, taps, env)

    vA_tok = v_evac[-1]



    PR = PSS[:, 1024:2048]
    rope_toks = {}
    wd_tok = [None, None]
    wd_reader = [None, None]
    prep_state = {"pr_free": xnT_tok, "swap_live": None, "proj_last": None, "qb_ready": None}

    def make_prep(cc):
        dmas = []
        jobs = []
        for qk in range(2):
            kind = "q" if qk == 0 else "k"
            d = qk * 8 + cc
            buf = (2 * cc + qk) % 2
            dest = qT if qk == 0 else kT

            def dma_job(d=d, buf=buf):
                w = [wd_reader[buf]] if wd_reader[buf] is not None else []
                wd_tok[buf] = S.emit("pool", lambda e, d=d, buf=buf: e.dma_start(
                    out=wd[:, 1024 * buf:1024 * (buf + 1)]
                    .rearrange("p (c n) -> p c n", c=NC),
                    in_=wqkv.ap()[:, 128 * d:128 * (d + 1)]
                    .rearrange("(c p) n -> p c n", p=128)),
                    waits=w, inc="dmag", amt=16)
            dmas.append(dma_job)
            for j in range(2):
                for cg in range(4):
                    def proj_job(cg=cg, j=j, buf=buf):
                        for c in (2 * cg, 2 * cg + 1):
                            for s2 in range(2):
                                w = []
                                if c == 0 and s2 == 0:
                                    w = [S.tok("dmag"), xnT_tok, xnT_oth,
                                         prep_state["pr_free"]]
                                t = S.emit(
                                    "pe", lambda e, c=c, s2=s2, j=j, buf=buf: e.matmul(
                                        PR[:, 512 * s2:512 * (s2 + 1)],
                                        wd[:, 1024 * buf + 128 * c:
                                           1024 * buf + 128 * (c + 1)],
                                        xnT.ap()[:, T * c + 1024 * j + 512 * s2:
                                                 T * c + 1024 * j + 512 * (s2 + 1)],
                                        start=(c == 0), stop=(c == NC - 1)),
                                    waits=w, inc="pe")
                                wd_reader[buf] = t
                                prep_state["proj_last"] = t
                    jobs.append(proj_job)

                def swap_job():
                    ev = S.emit("dve", lambda e: e.tensor_copy(
                        qb[:, 0:1024], PR[:, 0:1024]),
                        waits=[prep_state["proj_last"]], inc="dve")
                    prep_state["qb_ready"] = ev
                    S.emit("pe", lambda e: e.matmul(
                        PR[:, 0:512], rsw[:], qb[:, 0:512],
                        start=True, stop=True), waits=[ev, S.tok("dmas")])
                    prep_state["swap_live"] = S.emit("pe", lambda e: e.matmul(
                        PR[:, 512:1024], rsw[:], qb[:, 512:1024],
                        start=True, stop=True), inc="pe")
                jobs.append(swap_job)

                def rope_job(j=j, cc=cc, kind=kind, dest=dest):
                    sw2 = prep_state["swap_live"]
                    a_tok = S.emit("pool", lambda e, j=j: e.tensor_tensor(
                        poolb[:, 0:1024], qb[:, 0:1024],
                        C4[:, 1024 * j:1024 * (j + 1)], ALU.mult),
                        waits=[tables_tok, tables_dve, tables_act, vA_tok,
                               prep_state["qb_ready"]], inc="pool")
                    b_tok = S.emit("dve", lambda e, j=j: e.tensor_tensor(
                        poolb[:, 1024:2048], PR[:, 0:1024],
                        S4[:, 1024 * j:1024 * (j + 1)], ALU.mult),
                        waits=[sw2], inc="dve")
                    rt = S.emit("pool", lambda e, dest=dest, cc=cc, j=j:
                                e.tensor_tensor(
                                    dest[:, T * cc + 1024 * j:
                                         T * cc + 1024 * (j + 1)],
                                    poolb[:, 0:1024], poolb[:, 1024:2048],
                                    ALU.add), waits=[a_tok, b_tok], inc="pool")
                    rope_toks[(kind, cc, j)] = rt
                    prep_state["pr_free"] = b_tok
                jobs.append(rope_job)
        if cc == NC - 1:
            def wo_job():
                wo_holder["tok"] = S.emit("pool", lambda e: e.dma_start(
                    out=wo8.ap().rearrange("p (c n) -> p c n", c=NC),
                    in_=wo_d.ap().rearrange("(c p) n -> p c n", p=128)),
                    waits=[rope_toks[("k", NC - 1, 1)]], inc="dmag", amt=16)
            jobs.append(wo_job)
        return dmas, jobs

    def quarter_pieces():
        out = []
        for Q in range(4):
            for kt in range(min(16, 4 * Q + 4)):
                qs = max(512 * Q, 128 * kt)
                out.append((Q, kt, qs, 512 * (Q + 1)))
        return out

    PIECES = quarter_pieces()
    NP = len(PIECES)

    exp_tok = {}
    mask_tok = {}
    pv_tok = {}
    norm_tok = {}
    recip_tok = {}
    bcast_tok = {}
    for hh in range(2):
        for par in range(2):
            norm_tok[(hh, par)] = vA_tok
    pc_holder = [0]
    pending = []

    def pop_one():
        pending.pop(0)[2]()

    def run_chunk(cc, prep_dmas, prep_jobs):
        h_e, h_o = 2 * cc, 2 * cc + 1
        kw = [rope_toks[("k", cc, 0)], rope_toks[("k", cc, 1)],
              rope_toks[("q", cc, 0)], rope_toks[("q", cc, 1)], vA_tok]
        base = pc_holder[0]
        last_pv = {h_e: None, h_o: None}

        for i, (Q, kt, qs, qe) in enumerate(PIECES):
            p = base + i
            ln = qe - qs
            slot3 = p % 3
            par = Q % 2
            while (p - 3) >= 0 and (p - 3) not in pv_tok:
                assert pending, "pipeline stuck"
                pop_one()
            for hh in range(2):
                rb = 64 * hh
                w = list(kw)
                if (p - 1, hh) in exp_tok:
                    w.append(exp_tok[(p - 1, hh)])
                sm = S.emit("pe", lambda e, kt=kt, qs=qs, qe=qe, cc=cc, rb=rb, hh=hh: e.matmul(
                    PSS[:, 512 * hh:512 * hh + (qe - qs)],
                    kT.ap()[rb:rb + 64, T * cc + 128 * kt: T * cc + 128 * (kt + 1)],
                    qT.ap()[rb:rb + 64, T * cc + qs: T * cc + qe],
                    start=True, stop=True, tile_position=(rb, 0)),
                    waits=w, inc="pe")
                ew = [sm, bias_tok]
                if (p - 3, hh) in pv_tok:
                    ew.append(pv_tok[(p - 3, hh)])
                exp_tok[(p, hh)] = S.emit("act", lambda e, slot3=slot3, ln=ln, hh=hh: e.activation(
                    pt[:, 1024 * slot3 + 512 * hh: 1024 * slot3 + 512 * hh + ln],
                    PSS[:, 512 * hh:512 * hh + ln],
                    AF.Exp, bias=biasC[:, 0:1], scale=0.125), waits=ew, inc="act")
                if qs == 128 * kt:
                    mask_tok[(p, hh)] = S.emit("pool", lambda e, slot3=slot3, hh=hh: e.affine_select(
                        out=pt[:, 1024 * slot3 + 512 * hh:
                               1024 * slot3 + 512 * hh + 128],
                        in_=pt[:, 1024 * slot3 + 512 * hh:
                               1024 * slot3 + 512 * hh + 128],
                        pattern=[[1, 128]], compare_op=ALU.is_ge, fill=0.0,
                        channel_multiplier=-1, base=0),
                        waits=[exp_tok[(p, hh)]], inc="pool")

            def pv_job(p=p, Q=Q, kt=kt, qs=qs, qe=qe, slot3=slot3, par=par):
                stop = (kt == min(15, 4 * Q + 3))
                tok = None
                for hh, h in ((0, h_e), (1, h_o)):
                    w = [exp_tok[(p, hh)]]
                    if (p, hh) in mask_tok:
                        w.append(mask_tok[(p, hh)])
                    if kt == 0:
                        w.append(norm_tok[(hh, par)])
                    ycol = 1024 * hh + 512 * par
                    tok = S.emit("pe", lambda e, h=h, hh=hh, kt=kt, qs=qs, qe=qe,
                                 slot3=slot3, Q=Q, stop=stop, ycol=ycol: e.matmul(
                        PSA[0:128, ycol + qs - 512 * Q: ycol + qe - 512 * Q],
                        vA.ap()[:, (kt * NH + h) * 65:(kt * NH + h) * 65 + 128],
                        pt[:, 1024 * slot3 + 512 * hh:
                           1024 * slot3 + 512 * hh + (qe - qs)],
                        start=(kt == 0), stop=stop, skip_group_check=True),
                        waits=w, inc="pe")
                    last_pv[h] = tok
                    pv_tok[(p, hh)] = tok
                pv_tok[p] = tok
            pending.append(("pv", p, pv_job))


            if kt == min(15, 4 * Q + 3):
                for hh, h in ((0, h_e), (1, h_o)):
                    def recipB_job(hh=hh, h=h, par=par):
                        def _r(e):
                            with nc.allow_low_precision(
                                    reason="recip row feeds bf16 bcast"):
                                return e.reciprocal(
                                    rrow[64:65, 512 * (2 * hh + par):
                                         512 * (2 * hh + par) + 512],
                                    PSA[64:65, 1024 * hh + 512 * par:
                                        1024 * hh + 512 * par + 512])
                        recip_tok[(hh, par, 1)] = S.emit(
                            "dve", _r, waits=[last_pv[h]], inc="dve")
                    pending.append(("dve", None, recipB_job))
                for hh, h in ((0, h_e), (1, h_o)):
                    def bcastB_job(hh=hh, par=par):
                        bcast_tok[(hh, par, 1)] = S.emit(
                            "pe", lambda e, hh=hh, par=par: e.matmul(
                                PSA[64:128, 1024 * hh + 512 * par:
                                    1024 * hh + 512 * par + 512],
                                onesb[64:65, 0:64],
                                rrow[64:65, 512 * (2 * hh + par):
                                     512 * (2 * hh + par) + 512],
                                start=True, stop=True, skip_group_check=True),
                            waits=[recip_tok[(hh, par, 1)], ones_tok],
                            inc="pe")
                    pending.append(("pe", p + 5 + hh, bcastB_job))

                    def norm_job(hh=hh, Q=Q, cc=cc, par=par):
                        evb = S.emit("dve", lambda e, hh=hh, par=par: e.tensor_copy(
                            bcSB[:, 512 * hh:512 * (hh + 1)],
                            PSA[64:128, 1024 * hh + 512 * par:
                                1024 * hh + 512 * par + 512]),
                            waits=[bcast_tok[(hh, par, 1)]], inc="dve")
                        cs = slice(T * cc + 512 * Q, T * cc + 512 * (Q + 1))
                        if hh == 0:
                            nt = S.emit("dve", lambda e, hh=hh, par=par, cs=cs:
                                        e.tensor_tensor(
                                qT.ap()[0:64, cs],
                                PSA[0:64, 1024 * hh + 512 * par:
                                    1024 * hh + 512 * par + 512],
                                bcSB[:, 512 * hh:512 * (hh + 1)],
                                ALU.mult), waits=[evb], inc="dve")
                        else:
                            t1 = S.emit("dve", lambda e, hh=hh, par=par: e.tensor_tensor(
                                ytt[:, 0:512],
                                PSA[0:64, 1024 * hh + 512 * par:
                                    1024 * hh + 512 * par + 512],
                                bcSB[:, 512 * hh:512 * (hh + 1)],
                                ALU.mult), waits=[evb], inc="dve")
                            nt = S.emit("dve", lambda e, cs=cs: e.tensor_copy(
                                qT.ap()[64:128, cs], ytt[:, 0:512]),
                                waits=[t1], inc="dve")
                        norm_tok[(hh, par)] = nt
                    pending.append(("dve", None, norm_job))

            popped_pe = False
            while pending:
                tag, pp, _ = pending[0]
                if tag == "dve":
                    pop_one()
                elif tag == "pv" and pp <= p - 1:
                    pop_one()
                elif tag == "pe" and not popped_pe and (pp is None or pp <= p):
                    pop_one()
                    popped_pe = True
                else:
                    break
            if i < len(prep_dmas):
                prep_dmas[i]()
            elif i >= 4 and prep_jobs:
                prep_jobs.pop(0)()
                if i % 2 == 0 and prep_jobs:
                    prep_jobs.pop(0)()

        while prep_jobs:
            prep_jobs.pop(0)()
        pc_holder[0] = base + NP

    d0, j0 = make_prep(0)
    for f in d0 + j0:
        f()

    wo_holder = {}
    swh = {}

    def emit_sw():
        sw_toks = []
        for c in range(NC):
            par = c % 2
            ddw = [tables_tok, tables_act, tables_dve, S.tok("dve"), S.tok("pe")]
            if c >= 2:
                ddw.append(sw_toks[c - 2])
            S.emit("sync", lambda e, c=c, par=par: e.dma_start(
                out=p4(par), in_=wft.ap()[128 * c:128 * (c + 1), :]),
                waits=ddw, inc="dmas", amt=16)
            sw_toks.append(S.emit("act", lambda e, c=c, par=par: e.sign(
                swT[:, 1024 * c:1024 * (c + 1)], p4(par)), waits=[S.tok("dmas")],
                inc="act"))
        swh["tok"] = sw_toks[-1]

    for cc in range(NC):
        nd, nj = make_prep(cc + 1) if cc + 1 < NC else ([], [])
        run_chunk(cc, nd, nj)
        if cc == 0:
            emit_sw()
    while pending:
        pop_one()
    yT_tok = norm_tok[(1, 1)]
    yT_all = [norm_tok[k] for k in norm_tok]
    wo_tok = wo_holder["tok"]

    if upto == "ph3":
        return finish(nc, S, y, taps, env)

    regs = [PSA[:, 0:1024], PSA[:, 1024:2048], PSS[:, 0:1024], PSS[:, 1024:2048]]
    reg_free = [None, None, None, None]
    ph3_guard = yT_all + [S.tok("dve"), S.tok("act"), S.tok("pool")]
    swT_tok = swh["tok"]
    xr_tok = S.emit("sync", lambda e: e.dma_start(
        out=p4(1), in_=x.ap()[0:128, :]), waits=[swT_tok] + ph3_guard,
        inc="dmas", amt=16)
    g_free = None
    out_free = None
    for tau in range(NT):
        fr = (2 * tau) % 4
        wr = (2 * tau + 1) % 4
        ffn_tok = None
        for c in range(NC):
            for s2 in range(2):
                w = []
                if c == 0 and s2 == 0:
                    w = [swT_tok] + ph3_guard
                    if reg_free[fr] is not None:
                        w.append(reg_free[fr])
                ffn_tok = S.emit("pe", lambda e, c=c, s2=s2, fr=fr, tau=tau: e.matmul(
                    regs[fr][:, 512 * s2:512 * (s2 + 1)],
                    xnT.ap()[:, T * c + 128 * tau: T * c + 128 * (tau + 1)],
                    swT[:, 1024 * c + 512 * s2: 1024 * c + 512 * (s2 + 1)],
                    start=(c == 0), stop=(c == NC - 1)), waits=w, inc="pe")
        wo_mm = None
        for c in range(NC):
            for s2 in range(2):
                w = []
                if c == 0 and s2 == 0:
                    w = [S.tok("dmag"), yT_tok]
                    if reg_free[wr] is not None:
                        w.append(reg_free[wr])
                wo_mm = S.emit("pe", lambda e, c=c, s2=s2, wr=wr, tau=tau: e.matmul(
                    regs[wr][:, 512 * s2:512 * (s2 + 1)],
                    qT.ap()[:, T * c + 128 * tau: T * c + 128 * (tau + 1)],
                    wo8[:, 1024 * c + 512 * s2: 1024 * c + 512 * (s2 + 1)],
                    start=(c == 0), stop=(c == NC - 1)), waits=w, inc="pe")
        gw = [ffn_tok]
        if g_free is not None:
            gw.append(g_free)
        g_tok = S.emit("act", lambda e, fr=fr: e.activation(
            p4(0), regs[fr][:], AF.Silu), waits=gw, inc="act")
        reg_free[fr] = g_tok
        t_tok = S.emit("dve", lambda e, wr=wr: e.tensor_tensor(
            p4(2), regs[wr][:], p4(0), ALU.mult), waits=[wo_mm, g_tok],
            inc="dve")
        reg_free[wr] = t_tok
        g_free = t_tok
        ow = [t_tok, S.tok("dmas")]
        if out_free is not None:
            ow.append(out_free)
        o_tok = S.emit("dve", lambda e: e.tensor_tensor(
            p4(3), p4(2), p4(1), ALU.add), waits=ow, inc="dve")
        if tau + 1 < NT:
            xr_tok = S.emit("sync", lambda e, tau=tau: e.dma_start(
                out=p4(1), in_=x.ap()[128 * (tau + 1):128 * (tau + 2), :]),
                waits=[o_tok], inc="dmas", amt=16)
        out_free = S.emit("sync", lambda e, tau=tau: e.dma_start(
            out=y.ap()[128 * tau:128 * (tau + 1), :], in_=p4(3)),
            waits=[o_tok], inc="out", amt=16)

    return finish(nc, S, y, taps, env)


def finish(nc, S, y, taps, env):
    if taps:
        for (name, shape, dt, get_ap) in taps:
            od = nc.dram_tensor(name, shape, dt, kind="ExternalOutput")
            waits = [(k, S.cnt[k]) for k in ("pe", "act", "dve", "pool")]
            S.emit("sync", lambda e, od=od, get_ap=get_ap: e.dma_start(
                out=od.ap(), in_=get_ap(env)), waits=waits, inc="out", amt=16)
    final_waits = [(k, S.cnt[k]) for k in S.cnt if S.cnt[k] > 0]
    sem = S.sem

    def fin(e):
        for s, v in final_waits:
            e.wait_ge(sem[s], v)
    S.q["sync"].append(fin)

    with nc.Block() as block:
        @block.sync
        def _(e):
            for f in S.q["sync"]:
                f(e)

        @block.tensor
        def _(e):
            for f in S.q["pe"]:
                f(e)

        @block.scalar
        def _(e):
            for f in S.q["act"]:
                f(e)

        @block.vector
        def _(e):
            for f in S.q["dve"]:
                f(e)

        @block.gpsimd
        def _(e):
            for f in S.q["pool"]:
                f(e)
    return nc


def make_rsw():
    r = np.zeros((128, 128), dtype=np.float32)
    for m in range(128):
        r[(m + 32) if m % 64 < 32 else (m - 32), m] = 1.0
    return r.astype(bf16)


def kernel(L0_attn_x, L0_attn_W_qkv, L0_attn_W_o, L0_ffn_weight):
    x = np.ascontiguousarray(np.asarray(L0_attn_x, dtype=np.float32))
    wqkv = np.ascontiguousarray(np.asarray(L0_attn_W_qkv, dtype=np.float32))
    wo = np.ascontiguousarray(np.asarray(L0_attn_W_o, dtype=np.float32))
    wft = np.ascontiguousarray(np.asarray(L0_ffn_weight, dtype=np.float32).T)
    rsw = make_rsw()

    nc = build("full")
    in_maps = [{"x": np.ascontiguousarray(x[c]), "wqkv": wqkv, "wo": wo,
                "wft": wft, "rsw": rsw} for c in range(8)]
    res = run_bass_kernel_spmd(nc, in_maps, core_ids=list(range(8)))
    return np.stack([r["y"] for r in res.results], axis=0)


# version 28
# speedup vs baseline: 1.0452x; 1.0452x over previous
import numpy as np
import ml_dtypes
import concourse.bass as bass
import concourse.mybir as mybir
from concourse.bass_utils import run_bass_kernel_spmd

F32 = mybir.dt.float32
BF16 = mybir.dt.bfloat16
I32 = mybir.dt.int32
AF = mybir.ActivationFunctionType
ALU = mybir.AluOpType
AX = mybir.AxisListType
bf16 = ml_dtypes.bfloat16

D = 1024
T = 2048
NH = 16
DH = 64
NC = 8
NT = 16
PI = float(np.pi)
EXP_C = 30.0

ENGS = ("pe", "act", "dve", "pool", "sync")
OWN = {"pe": "pe", "act": "act", "dve": "dve", "pool": "pool"}


class Sched:

    def __init__(self, nc):
        self.nc = nc
        self.q = {e: [] for e in ENGS}
        names = ["pe", "act", "dve", "pool", "dmas", "dmag", "out",
                 "x0", "x1", "x2", "x3"]
        self.sem = {k: nc.alloc_semaphore("s_" + k) for k in names}
        self.cnt = {k: 0 for k in names}
        self.watermark = {e: {k: 0 for k in names} for e in ENGS}

    def emit(self, eng, fn, waits=(), inc=None, amt=1):
        ws = []
        for s, v in waits:
            if v is None:
                continue
            assert v <= self.cnt[s], f"future token {s}={v} (now {self.cnt[s]})"
            if v > self.watermark[eng][s]:
                self.watermark[eng][s] = v
                ws.append((s, v))
        tok = None
        if inc is not None:
            self.cnt[inc] += amt
            tok = (inc, self.cnt[inc])
            if OWN.get(eng) == inc:
                self.watermark[eng][inc] = self.cnt[inc]
        sem = self.sem

        def run(e, fn=fn, ws=ws, inc=inc, amt=amt):
            for s, v in ws:
                e.wait_ge(sem[s], v)
            inst = fn(e)
            if inc is not None:
                inst.then_inc(sem[inc], amt)

        self.q[eng].append(run)
        return tok

    def tok(self, s):
        return (s, self.cnt[s])


def build(upto="full", taps=None):
    nc = bass.Bass("TRN2", target_bir_lowering=False)

    x = nc.dram_tensor("x", [T, D], F32, kind="ExternalInput")
    wqkv = nc.dram_tensor("wqkv", [D, 3 * D], F32, kind="ExternalInput")
    wo_d = nc.dram_tensor("wo", [D, D], F32, kind="ExternalInput")
    wft = nc.dram_tensor("wft", [D, D], F32, kind="ExternalInput")
    rsw_d = nc.dram_tensor("rsw", [128, 128], BF16, kind="ExternalInput")
    y = nc.dram_tensor("y", [T, D], F32, kind="ExternalOutput")

    xnT = nc.alloc_sbuf_tensor("xnT", [128, NC * T], BF16)
    qT = nc.alloc_sbuf_tensor("qT", [128, NC * T], BF16)
    kT = nc.alloc_sbuf_tensor("kT", [128, NC * T], BF16)
    vA = nc.alloc_sbuf_tensor("vA", [128, NT * NH * 65 + 64], BF16)
    wo8 = nc.alloc_sbuf_tensor("wo8", [128, 8192], BF16)
    swT = nc.alloc_sbuf_tensor("swT", [128, 8192], BF16)
    wd = nc.alloc_sbuf_tensor("wd", [128, 2048], BF16)
    qb = nc.alloc_sbuf_tensor("qb", [128, 1024], BF16)
    pt = nc.alloc_sbuf_tensor("pt", [128, 3072], BF16)
    rrow = nc.alloc_sbuf_tensor("rrow", [128, 2048], BF16)
    rrow32 = nc.alloc_sbuf_tensor("rrow32", [128, 512], F32)
    bcSB = nc.alloc_sbuf_tensor("bcSB", [64, 1024], BF16)
    ytt = nc.alloc_sbuf_tensor("ytt", [64, 1024], BF16)
    pool4 = nc.alloc_sbuf_tensor("pool4", [128, 4096], F32)
    poolb = nc.alloc_sbuf_tensor("poolb", [128, 4096], BF16)
    identf = nc.alloc_sbuf_tensor("identf", [128, 128], F32)
    rsw = nc.alloc_sbuf_tensor("rswb", [128, 128], BF16)
    onesb = nc.alloc_sbuf_tensor("onesb", [128, 64], BF16)
    biasC = nc.alloc_sbuf_tensor("biasC", [128, 4], F32)
    stats = nc.alloc_sbuf_tensor("stats", [128, 12], F32)
    mv = nc.alloc_sbuf_tensor("mv", [128, 2], F32)
    lncol = nc.alloc_sbuf_tensor("lncol", [128, 48], F32)
    ecol = nc.alloc_sbuf_tensor("ecol", [128, 1], F32)
    trow = swT.ap()[:, 0:4096].bitcast(F32)

    C4 = wo8[:, 0:2048]
    S4 = wo8[:, 2048:4096]

    def p4(i):
        return pool4[:, 1024 * i:1024 * (i + 1)]

    PSA = nc.alloc_psum_tensor("PSA", [128, 2048], F32)
    PSS = nc.alloc_psum_tensor("PSS", [128, 2048], F32)

    S = Sched(nc)
    env = {"xnT": xnT, "qT": qT, "kT": kT, "vA": vA, "wo8": wo8, "swT": swT,
           "pt": pt, "rrow": rrow, "pool4": pool4, "poolb": poolb, "trow": trow,
           "stats": stats, "mv": mv, "lncol": lncol, "qb": qb, "bcSB": bcSB,
           "ecol": ecol, "identf": identf, "onesb": onesb, "biasC": biasC}

    rsw_tok = S.emit("sync", lambda e: e.dma_start(out=rsw[:], in_=rsw_d.ap()),
                     inc="dmas", amt=16)

    S.emit("pool", lambda e: e.memset(identf[:], 1.0))
    id_tok = S.emit("pool", lambda e: e.affine_select(
        out=identf[:], in_=identf[:], pattern=[[-1, 128]],
        compare_op=ALU.is_equal, fill=0.0, channel_multiplier=1, base=0),
        inc="pool")
    ones_tok = S.emit("pool", lambda e: e.memset(onesb[:], 1.0), inc="pool")
    S.emit("pool", lambda e: e.memset(vA.ap()[:, NT * NH * 65:], 0.0))
    vones_tok = S.emit("pool", lambda e: e.memset(
        vA.ap()[:, 64:NT * NH * 65:65], 1.0), inc="pool")
    bias_tok = S.emit("dve", lambda e: e.memset(biasC[:, 0:1], -EXP_C), inc="dve")
    S.emit("dve", lambda e: e.memset(ecol[:], 1.0))
    S.emit("dve", lambda e: e.memset(ecol[0:32, :], -1.0))
    ecol_tok = S.emit("dve", lambda e: e.memset(ecol[64:96, :], -1.0), inc="dve")



    sqdump = xnT.ap()[:, 0:2048].bitcast(F32)
    red_toks = []
    xslot_tok = {}
    for tau in range(4):
        xslot_tok[tau % 4] = S.emit("sync", lambda e, tau=tau: e.dma_start(
            out=p4(tau), in_=x.ap()[128 * tau:128 * (tau + 1), :]),
            waits=[], inc="x%d" % (tau % 4), amt=16)
    redA = None
    for tau in range(NT):
        par = tau % 4
        xt = p4(par)
        S.emit("act", lambda e, xt=xt, tau=tau: e.activation(
            sqdump, xt, AF.Square, accum_out=lncol[:, 16 + tau:17 + tau]),
            waits=[xslot_tok[par]])
        redA = S.emit("dve", lambda e, xt=xt, tau=tau: e.reduce_sum(
            lncol[:, tau:tau + 1], xt, AX.X), waits=[xslot_tok[par], S.tok("act")],
            inc="dve")
        red_toks.append(redA)
        if tau + 4 < NT:
            xslot_tok[par] = S.emit("sync", lambda e, tau=tau, par=par: e.dma_start(
                out=p4(par), in_=x.ap()[128 * (tau + 4):128 * (tau + 5), :]),
                waits=[redA], inc="x%d" % par, amt=16)
    trow2 = swT.ap()[:, 4096:8192].bitcast(F32)
    it_tok = S.emit("pool", lambda e: e.iota(
        trow[0:1, :], [[1, 2048]], base=0, channel_multiplier=0,
        allow_small_or_imprecise_dtypes=True), inc="pool")
    S.emit("pool", lambda e: e.iota(
        trow2[0:1, 0:128].rearrange("a (r i) -> a r i", i=32),
        [[0, 4], [1, 32]], base=0, channel_multiplier=0,
        allow_small_or_imprecise_dtypes=True), inc="pool")
    S.emit("act", lambda e: e.activation(
        trow2[0:1, 0:128], trow2[0:1, 0:128], AF.Exp,
        scale=-float(np.log(10000.0)) / 32.0), waits=[S.tok("pool")])
    th_tok = S.emit("act", lambda e: e.drain(), inc="act")

    ang_tok = None
    for j in range(2):
        for s2 in range(2):
            cs = slice(1024 * j + 512 * s2, 1024 * j + 512 * s2 + 512)
            ang_tok = S.emit("pe", lambda e, cs=cs: e.matmul(
                PSA[:, cs], trow2[0:1, 0:128], trow[0:1, cs],
                start=True, stop=True), waits=[th_tok, it_tok], inc="pe")

    tgA = poolb.ap()[:, 0:2048].bitcast(F32)
    tgB = poolb.ap()[:, 2048:4096].bitcast(F32)
    sin_last = None
    for (dest, quarter, signed) in [(C4, True, False), (S4, False, True)]:
        for j in range(2):
            cs = slice(1024 * j, 1024 * (j + 1))
            w0 = [ang_tok] + ([sin_last] if sin_last else [])
            S.emit("dve", lambda e, cs=cs, quarter=quarter: e.tensor_scalar(
                tgA, PSA[:, cs], 1.0 / (2 * PI), 0.25 if quarter else 0.0,
                ALU.mult, ALU.add), waits=w0)
            S.emit("dve", lambda e: e.tensor_copy(tgB.bitcast(I32), tgA))
            S.emit("dve", lambda e: e.tensor_copy(tgA, tgB.bitcast(I32)))
            r_tok = S.emit("dve", lambda e, cs=cs: e.scalar_tensor_tensor(
                tgB, tgA, -2 * PI, PSA[:, cs], ALU.mult, ALU.add), inc="dve")
            if quarter:
                r_tok = S.emit("dve", lambda e: e.tensor_scalar_add(
                    tgB, tgB, PI / 2), inc="dve")
            if signed:
                s_tok = S.emit("act", lambda e: e.activation(
                    tgA, tgB, AF.Sin), waits=[r_tok], inc="act")
                sin_last = S.emit("dve", lambda e, dest=dest, cs=cs: e.tensor_scalar_mul(
                    dest[:, cs], tgA, ecol[:, 0:1]), waits=[s_tok, ecol_tok],
                    inc="dve")
            else:
                sin_last = S.emit("act", lambda e, dest=dest, cs=cs: e.activation(
                    dest[:, cs], tgB, AF.Sin), waits=[r_tok], inc="act")
    tables_tok = sin_last
    tables_dve = S.tok("dve")
    tables_act = S.tok("act")
    aA = S.emit("act", lambda e: e.drain(), inc="act")
    S.emit("dve", lambda e: e.drain(), waits=[aA])
    S.emit("dve", lambda e: e.tensor_scalar_mul(
        lncol[:, 0:16], lncol[:, 0:16], -1.0 / D))
    S.emit("dve", lambda e: e.tensor_scalar(
        lncol[:, 16:32], lncol[:, 16:32], 1.0 / D, 1e-5, ALU.mult, ALU.add))
    S.emit("dve", lambda e: e.drain())
    S.emit("dve", lambda e: e.tensor_tensor(
        lncol[:, 32:48], lncol[:, 0:16], lncol[:, 0:16], ALU.mult))
    S.emit("dve", lambda e: e.drain())
    S.emit("dve", lambda e: e.tensor_tensor(
        lncol[:, 16:32], lncol[:, 16:32], lncol[:, 32:48], ALU.subtract))
    vB = S.emit("dve", lambda e: e.drain(), inc="dve")
    S.emit("act", lambda e: e.activation(
        lncol[:, 32:48], lncol[:, 16:32], AF.Sqrt), waits=[vB])
    aB = S.emit("act", lambda e: e.drain(), inc="act")
    S.emit("dve", lambda e: e.reciprocal(lncol[:, 16:32], lncol[:, 32:48]),
           waits=[aB])
    S.emit("dve", lambda e: e.drain())
    S.emit("dve", lambda e: e.tensor_tensor(
        lncol[:, 32:48], lncol[:, 0:16], lncol[:, 16:32], ALU.mult))
    ln_tok = S.emit("dve", lambda e: e.drain(), inc="dve")

    evac_toks = [None, None]
    v_evac = []
    xnbufs = [pt.ap()[:, 0:2048].bitcast(F32),
              rrow.ap()[:, 0:2048].bitcast(F32)]
    wvfull = swT.ap()[:, 0:8192]
    wv_toks = [None, None]
    wv_toks[0] = S.emit("pool", lambda e: e.dma_start(
        out=wvfull.rearrange("p (c n) -> p c n", c=NC),
        in_=wqkv.ap()[:, 2048:3072].rearrange("(c p) n -> p c n", p=128)),
        waits=[tables_tok, tables_dve, tables_act], inc="dmag", amt=16)
    for tau in range(4):
        xslot_tok[tau % 4] = S.emit("sync", lambda e, tau=tau: e.dma_start(
            out=p4(tau), in_=x.ap()[128 * tau:128 * (tau + 1), :]),
            waits=[red_toks[min(tau + 12, NT - 1)]], inc="x%d" % (tau % 4),
            amt=16)
    tp_tok = None
    tp_prev = [None, None]
    vjobs = []
    for tau in range(NT):
        par = tau % 4
        xt = p4(par)
        xn = xnbufs[tau % 2]
        xw = [xslot_tok[par], ln_tok]
        if tp_prev[tau % 2] is not None:
            xw.append(tp_prev[tau % 2])
        xn_tok = S.emit("dve", lambda e, xt=xt, xn=xn, tau=tau: e.tensor_scalar(
            xn, xt, lncol[:, 16 + tau:17 + tau], lncol[:, 32 + tau:33 + tau],
            ALU.mult, ALU.add), waits=xw, inc="dve")
        if tau + 4 < NT:
            xslot_tok[par] = S.emit("sync", lambda e, tau=tau, par=par: e.dma_start(
                out=p4(par), in_=x.ap()[128 * (tau + 4):128 * (tau + 5), :]),
                waits=[xn_tok], inc="x%d" % par, amt=16)
        bpar = tau % 2
        pcols = 1024 * bpar
        for c in range(NC):
            w = [xn_tok, id_tok]
            if c == 0 and evac_toks[bpar] is not None:
                w.append(evac_toks[bpar])
            tp_tok = S.emit("pe", lambda e, xn=xn, c=c, pcols=pcols: e.transpose(
                PSS[:, pcols + 128 * c: pcols + 128 * (c + 1)],
                xn[:, 128 * c:128 * (c + 1)], identf[:]), waits=w,
                inc="pe" if c == NC - 1 else None) or tp_tok
        tp_prev[tau % 2] = tp_tok

        def mk_out(tau=tau):
            return xnT.ap().rearrange("p (c t) -> p c t", c=NC)[:, :, 128 * tau:128 * (tau + 1)]
        evac_toks[bpar] = S.emit("dve", lambda e, pcols=pcols, mo=mk_out: e.tensor_copy(
            mo(), PSS[:, pcols:pcols + 1024].rearrange("p (c r) -> p c r", c=NC)),
            waits=[tp_tok], inc="dve")

        def vjob(tau=tau, ev=evac_toks[bpar]):
            for vh in range(2):
                pcol = 512 * ((2 * tau + vh) % 4)
                mm_tok = None
                for c in range(NC):
                    w = []
                    if c == 0:
                        w = [S.tok("dmag"), ev]
                        if len(v_evac) >= 4:
                            w.append(v_evac[-4])
                    mm_tok = S.emit("pe", lambda e, c=c, tau=tau, pcol=pcol, vh=vh: e.matmul(
                        PSA[:, pcol:pcol + 512],
                        xnT.ap()[:, T * c + 128 * tau: T * c + 128 * (tau + 1)],
                        wvfull[:, 1024 * c + 512 * vh:1024 * c + 512 * vh + 512],
                        start=(c == 0), stop=(c == NC - 1)), waits=w,
                        inc="pe" if c == NC - 1 else None) or mm_tok

                def mk_vout(tau=tau, vh=vh):
                    a = vA.ap()[:, 0:NT * NH * 65].rearrange("p (t h s) -> p t h s", t=NT, h=NH)
                    return a[:, tau, 8 * vh:8 * vh + 8, 0:64]
                v_evac.append(S.emit("dve", lambda e, pcol=pcol, mo=mk_vout: e.tensor_copy(
                    mo(), PSA[:, pcol:pcol + 512].rearrange("p (h s) -> p h s", h=8)),
                    waits=[mm_tok, vones_tok], inc="dve"))
        vjobs.append(vjob)
        if tau >= 1:
            vjobs.pop(0)()
    while vjobs:
        vjobs.pop(0)()
    xnT_tok = evac_toks[(NT - 1) % 2]
    xnT_oth = evac_toks[NT % 2]

    if upto == "ph1":
        return finish(nc, S, y, taps, env)

    vA_tok = v_evac[-1]



    PR = PSS[:, 1024:2048]
    rope_toks = {}
    wd_tok = [None, None]
    wd_reader = [None, None]
    prep_state = {"pr_free": xnT_tok, "swap_live": None, "proj_last": None, "qb_ready": None}

    def make_prep(cc):
        dmas = []
        jobs = []
        for qk in range(2):
            kind = "q" if qk == 0 else "k"
            d = qk * 8 + cc
            buf = (2 * cc + qk) % 2
            dest = qT if qk == 0 else kT

            def dma_job(d=d, buf=buf):
                w = [wd_reader[buf]] if wd_reader[buf] is not None else []
                wd_tok[buf] = S.emit("pool", lambda e, d=d, buf=buf: e.dma_start(
                    out=wd[:, 1024 * buf:1024 * (buf + 1)]
                    .rearrange("p (c n) -> p c n", c=NC),
                    in_=wqkv.ap()[:, 128 * d:128 * (d + 1)]
                    .rearrange("(c p) n -> p c n", p=128)),
                    waits=w, inc="dmag", amt=16)
            dmas.append(dma_job)
            for j in range(2):
                for cg in range(4):
                    def proj_job(cg=cg, j=j, buf=buf):
                        for c in (2 * cg, 2 * cg + 1):
                            for s2 in range(2):
                                w = []
                                if c == 0 and s2 == 0:
                                    w = [S.tok("dmag"), xnT_tok, xnT_oth,
                                         prep_state["pr_free"]]
                                last = (c == NC - 1 and s2 == 1)
                                t = S.emit(
                                    "pe", lambda e, c=c, s2=s2, j=j, buf=buf: e.matmul(
                                        PR[:, 512 * s2:512 * (s2 + 1)],
                                        wd[:, 1024 * buf + 128 * c:
                                           1024 * buf + 128 * (c + 1)],
                                        xnT.ap()[:, T * c + 1024 * j + 512 * s2:
                                                 T * c + 1024 * j + 512 * (s2 + 1)],
                                        start=(c == 0), stop=(c == NC - 1)),
                                    waits=w, inc="pe" if last else None)
                                if last:
                                    wd_reader[buf] = t
                                    prep_state["proj_last"] = t
                    jobs.append(proj_job)

                def swap_job():
                    ev = S.emit("dve", lambda e: e.tensor_copy(
                        qb[:, 0:1024], PR[:, 0:1024]),
                        waits=[prep_state["proj_last"]], inc="dve")
                    prep_state["qb_ready"] = ev
                    S.emit("pe", lambda e: e.matmul(
                        PR[:, 0:512], rsw[:], qb[:, 0:512],
                        start=True, stop=True), waits=[ev, S.tok("dmas")])
                    prep_state["swap_live"] = S.emit("pe", lambda e: e.matmul(
                        PR[:, 512:1024], rsw[:], qb[:, 512:1024],
                        start=True, stop=True), inc="pe")
                jobs.append(swap_job)

                def rope_job(j=j, cc=cc, kind=kind, dest=dest):
                    sw2 = prep_state["swap_live"]
                    a_tok = S.emit("pool", lambda e, j=j: e.tensor_tensor(
                        poolb[:, 0:1024], qb[:, 0:1024],
                        C4[:, 1024 * j:1024 * (j + 1)], ALU.mult),
                        waits=[tables_tok, tables_dve, tables_act, vA_tok,
                               prep_state["qb_ready"]], inc="pool")
                    b_tok = S.emit("dve", lambda e, j=j: e.tensor_tensor(
                        poolb[:, 1024:2048], PR[:, 0:1024],
                        S4[:, 1024 * j:1024 * (j + 1)], ALU.mult),
                        waits=[sw2], inc="dve")
                    rt = S.emit("pool", lambda e, dest=dest, cc=cc, j=j:
                                e.tensor_tensor(
                                    dest[:, T * cc + 1024 * j:
                                         T * cc + 1024 * (j + 1)],
                                    poolb[:, 0:1024], poolb[:, 1024:2048],
                                    ALU.add), waits=[a_tok, b_tok], inc="pool")
                    rope_toks[(kind, cc, j)] = rt
                    prep_state["pr_free"] = b_tok
                jobs.append(rope_job)
        if cc == NC - 1:
            def wo_job():
                wo_holder["tok"] = S.emit("pool", lambda e: e.dma_start(
                    out=wo8.ap().rearrange("p (c n) -> p c n", c=NC),
                    in_=wo_d.ap().rearrange("(c p) n -> p c n", p=128)),
                    waits=[rope_toks[("k", NC - 1, 1)]], inc="dmag", amt=16)
            jobs.append(wo_job)
        return dmas, jobs

    def quarter_pieces():
        out = []
        for Q in range(4):
            for kt in range(min(16, 4 * Q + 4)):
                qs = max(512 * Q, 128 * kt)
                out.append((Q, kt, qs, 512 * (Q + 1)))
        return out

    PIECES = quarter_pieces()
    NP = len(PIECES)

    exp_tok = {}
    mask_tok = {}
    pv_tok = {}
    norm_tok = {}
    recip_tok = {}
    bcast_tok = {}
    for hh in range(2):
        for par in range(2):
            norm_tok[(hh, par)] = vA_tok
    pc_holder = [0]
    pending = []

    def pop_one():
        pending.pop(0)[2]()

    def run_chunk(cc, prep_dmas, prep_jobs):
        h_e, h_o = 2 * cc, 2 * cc + 1
        kw = [rope_toks[("k", cc, 0)], rope_toks[("k", cc, 1)],
              rope_toks[("q", cc, 0)], rope_toks[("q", cc, 1)], vA_tok]
        base = pc_holder[0]
        last_pv = {h_e: None, h_o: None}

        for i, (Q, kt, qs, qe) in enumerate(PIECES):
            p = base + i
            ln = qe - qs
            slot3 = p % 3
            par = Q % 2
            while (p - 3) >= 0 and (p - 3) not in pv_tok:
                assert pending, "pipeline stuck"
                pop_one()
            for hh in range(2):
                rb = 64 * hh
                w = list(kw)
                if (p - 1, hh) in exp_tok:
                    w.append(exp_tok[(p - 1, hh)])
                sm = S.emit("pe", lambda e, kt=kt, qs=qs, qe=qe, cc=cc, rb=rb, hh=hh: e.matmul(
                    PSS[:, 512 * hh:512 * hh + (qe - qs)],
                    kT.ap()[rb:rb + 64, T * cc + 128 * kt: T * cc + 128 * (kt + 1)],
                    qT.ap()[rb:rb + 64, T * cc + qs: T * cc + qe],
                    start=True, stop=True, tile_position=(rb, 0)),
                    waits=w, inc="pe")
                ew = [sm, bias_tok]
                if (p - 3, hh) in pv_tok:
                    ew.append(pv_tok[(p - 3, hh)])
                exp_tok[(p, hh)] = S.emit("act", lambda e, slot3=slot3, ln=ln, hh=hh: e.activation(
                    pt[:, 1024 * slot3 + 512 * hh: 1024 * slot3 + 512 * hh + ln],
                    PSS[:, 512 * hh:512 * hh + ln],
                    AF.Exp, bias=biasC[:, 0:1], scale=0.125), waits=ew, inc="act")
                if qs == 128 * kt:
                    mask_tok[(p, hh)] = S.emit("pool", lambda e, slot3=slot3, hh=hh: e.affine_select(
                        out=pt[:, 1024 * slot3 + 512 * hh:
                               1024 * slot3 + 512 * hh + 128],
                        in_=pt[:, 1024 * slot3 + 512 * hh:
                               1024 * slot3 + 512 * hh + 128],
                        pattern=[[1, 128]], compare_op=ALU.is_ge, fill=0.0,
                        channel_multiplier=-1, base=0),
                        waits=[exp_tok[(p, hh)]], inc="pool")

            def pv_job(p=p, Q=Q, kt=kt, qs=qs, qe=qe, slot3=slot3, par=par):
                stop = (kt == min(15, 4 * Q + 3))
                tok = None
                for hh, h in ((0, h_e), (1, h_o)):
                    w = [exp_tok[(p, hh)]]
                    if (p, hh) in mask_tok:
                        w.append(mask_tok[(p, hh)])
                    if kt == 0:
                        w.append(norm_tok[(hh, par)])
                    ycol = 1024 * hh + 512 * par
                    tok = S.emit("pe", lambda e, h=h, hh=hh, kt=kt, qs=qs, qe=qe,
                                 slot3=slot3, Q=Q, stop=stop, ycol=ycol: e.matmul(
                        PSA[0:128, ycol + qs - 512 * Q: ycol + qe - 512 * Q],
                        vA.ap()[:, (kt * NH + h) * 65:(kt * NH + h) * 65 + 128],
                        pt[:, 1024 * slot3 + 512 * hh:
                           1024 * slot3 + 512 * hh + (qe - qs)],
                        start=(kt == 0), stop=stop, skip_group_check=True),
                        waits=w, inc="pe")
                    last_pv[h] = tok
                    pv_tok[(p, hh)] = tok
                pv_tok[p] = tok
            pending.append(("pv", p, pv_job))


            if kt == min(15, 4 * Q + 3):
                for hh, h in ((0, h_e), (1, h_o)):
                    def recipB_job(hh=hh, h=h, par=par):
                        def _r(e):
                            with nc.allow_low_precision(
                                    reason="recip row feeds bf16 bcast"):
                                return e.reciprocal(
                                    rrow[64:65, 512 * (2 * hh + par):
                                         512 * (2 * hh + par) + 512],
                                    PSA[64:65, 1024 * hh + 512 * par:
                                        1024 * hh + 512 * par + 512])
                        recip_tok[(hh, par, 1)] = S.emit(
                            "dve", _r, waits=[last_pv[h]], inc="dve")
                    pending.append(("dve", None, recipB_job))
                for hh, h in ((0, h_e), (1, h_o)):
                    def bcastB_job(hh=hh, par=par):
                        bcast_tok[(hh, par, 1)] = S.emit(
                            "pe", lambda e, hh=hh, par=par: e.matmul(
                                PSA[64:128, 1024 * hh + 512 * par:
                                    1024 * hh + 512 * par + 512],
                                onesb[64:65, 0:64],
                                rrow[64:65, 512 * (2 * hh + par):
                                     512 * (2 * hh + par) + 512],
                                start=True, stop=True, skip_group_check=True),
                            waits=[recip_tok[(hh, par, 1)], ones_tok],
                            inc="pe")
                    pending.append(("pe", p + 5 + hh, bcastB_job))

                    def norm_job(hh=hh, Q=Q, cc=cc, par=par):
                        evb = S.emit("dve", lambda e, hh=hh, par=par: e.tensor_copy(
                            bcSB[:, 512 * hh:512 * (hh + 1)],
                            PSA[64:128, 1024 * hh + 512 * par:
                                1024 * hh + 512 * par + 512]),
                            waits=[bcast_tok[(hh, par, 1)]], inc="dve")
                        cs = slice(T * cc + 512 * Q, T * cc + 512 * (Q + 1))
                        if hh == 0:
                            nt = S.emit("dve", lambda e, hh=hh, par=par, cs=cs:
                                        e.tensor_tensor(
                                qT.ap()[0:64, cs],
                                PSA[0:64, 1024 * hh + 512 * par:
                                    1024 * hh + 512 * par + 512],
                                bcSB[:, 512 * hh:512 * (hh + 1)],
                                ALU.mult), waits=[evb], inc="dve")
                        else:
                            t1 = S.emit("dve", lambda e, hh=hh, par=par: e.tensor_tensor(
                                ytt[:, 0:512],
                                PSA[0:64, 1024 * hh + 512 * par:
                                    1024 * hh + 512 * par + 512],
                                bcSB[:, 512 * hh:512 * (hh + 1)],
                                ALU.mult), waits=[evb], inc="dve")
                            nt = S.emit("dve", lambda e, cs=cs: e.tensor_copy(
                                qT.ap()[64:128, cs], ytt[:, 0:512]),
                                waits=[t1], inc="dve")
                        norm_tok[(hh, par)] = nt
                    pending.append(("dve", None, norm_job))

            popped_pe = False
            while pending:
                tag, pp, _ = pending[0]
                if tag == "dve":
                    pop_one()
                elif tag == "pv" and pp <= p - 1:
                    pop_one()
                elif tag == "pe" and not popped_pe and (pp is None or pp <= p):
                    pop_one()
                    popped_pe = True
                else:
                    break
            if i < len(prep_dmas):
                prep_dmas[i]()
            elif i >= 6 and prep_jobs:
                prep_jobs.pop(0)()

        while prep_jobs:
            prep_jobs.pop(0)()
        pc_holder[0] = base + NP

    d0, j0 = make_prep(0)
    for f in d0 + j0:
        f()

    wo_holder = {}
    swh = {}

    def emit_sw():
        sw_toks = []
        for c in range(NC):
            par = c % 2
            ddw = [tables_tok, tables_act, tables_dve, S.tok("dve"), S.tok("pe")]
            if c >= 2:
                ddw.append(sw_toks[c - 2])
            S.emit("sync", lambda e, c=c, par=par: e.dma_start(
                out=p4(par), in_=wft.ap()[128 * c:128 * (c + 1), :]),
                waits=ddw, inc="dmas", amt=16)
            sw_toks.append(S.emit("act", lambda e, c=c, par=par: e.sign(
                swT[:, 1024 * c:1024 * (c + 1)], p4(par)), waits=[S.tok("dmas")],
                inc="act"))
        swh["tok"] = sw_toks[-1]

    for cc in range(NC):
        nd, nj = make_prep(cc + 1) if cc + 1 < NC else ([], [])
        run_chunk(cc, nd, nj)
        if cc == 0:
            emit_sw()
    while pending:
        pop_one()
    yT_tok = norm_tok[(1, 1)]
    yT_all = [norm_tok[k] for k in norm_tok]
    wo_tok = wo_holder["tok"]

    if upto == "ph3":
        return finish(nc, S, y, taps, env)

    regs = [PSA[:, 0:1024], PSA[:, 1024:2048], PSS[:, 0:1024], PSS[:, 1024:2048]]
    reg_free = [None, None, None, None]
    ph3_guard = yT_all + [S.tok("dve"), S.tok("act"), S.tok("pool")]
    swT_tok = swh["tok"]
    xr_tok = S.emit("sync", lambda e: e.dma_start(
        out=p4(1), in_=x.ap()[0:128, :]), waits=[swT_tok] + ph3_guard,
        inc="dmas", amt=16)
    g_free = None
    out_free = None
    for tau in range(NT):
        fr = (2 * tau) % 4
        wr = (2 * tau + 1) % 4
        ffn_tok = None
        for c in range(NC):
            for s2 in range(2):
                w = []
                if c == 0 and s2 == 0:
                    w = [swT_tok] + ph3_guard
                    if reg_free[fr] is not None:
                        w.append(reg_free[fr])
                ffn_tok = S.emit("pe", lambda e, c=c, s2=s2, fr=fr, tau=tau: e.matmul(
                    regs[fr][:, 512 * s2:512 * (s2 + 1)],
                    xnT.ap()[:, T * c + 128 * tau: T * c + 128 * (tau + 1)],
                    swT[:, 1024 * c + 512 * s2: 1024 * c + 512 * (s2 + 1)],
                    start=(c == 0), stop=(c == NC - 1)), waits=w,
                    inc="pe" if (c == NC - 1 and s2 == 1) else None) or ffn_tok
        wo_mm = None
        for c in range(NC):
            for s2 in range(2):
                w = []
                if c == 0 and s2 == 0:
                    w = [S.tok("dmag"), yT_tok]
                    if reg_free[wr] is not None:
                        w.append(reg_free[wr])
                wo_mm = S.emit("pe", lambda e, c=c, s2=s2, wr=wr, tau=tau: e.matmul(
                    regs[wr][:, 512 * s2:512 * (s2 + 1)],
                    qT.ap()[:, T * c + 128 * tau: T * c + 128 * (tau + 1)],
                    wo8[:, 1024 * c + 512 * s2: 1024 * c + 512 * (s2 + 1)],
                    start=(c == 0), stop=(c == NC - 1)), waits=w,
                    inc="pe" if (c == NC - 1 and s2 == 1) else None) or wo_mm
        gw = [ffn_tok]
        if g_free is not None:
            gw.append(g_free)
        g_tok = S.emit("act", lambda e, fr=fr: e.activation(
            p4(0), regs[fr][:], AF.Silu), waits=gw, inc="act")
        reg_free[fr] = g_tok
        t_tok = S.emit("dve", lambda e, wr=wr: e.tensor_tensor(
            p4(2), regs[wr][:], p4(0), ALU.mult), waits=[wo_mm, g_tok],
            inc="dve")
        reg_free[wr] = t_tok
        g_free = t_tok
        ow = [t_tok, S.tok("dmas")]
        if out_free is not None:
            ow.append(out_free)
        o_tok = S.emit("dve", lambda e: e.tensor_tensor(
            p4(3), p4(2), p4(1), ALU.add), waits=ow, inc="dve")
        if tau + 1 < NT:
            xr_tok = S.emit("sync", lambda e, tau=tau: e.dma_start(
                out=p4(1), in_=x.ap()[128 * (tau + 1):128 * (tau + 2), :]),
                waits=[o_tok], inc="dmas", amt=16)
        out_free = S.emit("sync", lambda e, tau=tau: e.dma_start(
            out=y.ap()[128 * tau:128 * (tau + 1), :], in_=p4(3)),
            waits=[o_tok], inc="out", amt=16)

    return finish(nc, S, y, taps, env)


def finish(nc, S, y, taps, env):
    if taps:
        for (name, shape, dt, get_ap) in taps:
            od = nc.dram_tensor(name, shape, dt, kind="ExternalOutput")
            waits = [(k, S.cnt[k]) for k in ("pe", "act", "dve", "pool")]
            S.emit("sync", lambda e, od=od, get_ap=get_ap: e.dma_start(
                out=od.ap(), in_=get_ap(env)), waits=waits, inc="out", amt=16)
    final_waits = [(k, S.cnt[k]) for k in S.cnt if S.cnt[k] > 0]
    sem = S.sem

    def fin(e):
        for s, v in final_waits:
            e.wait_ge(sem[s], v)
    S.q["sync"].append(fin)

    with nc.Block() as block:
        @block.sync
        def _(e):
            for f in S.q["sync"]:
                f(e)

        @block.tensor
        def _(e):
            for f in S.q["pe"]:
                f(e)

        @block.scalar
        def _(e):
            for f in S.q["act"]:
                f(e)

        @block.vector
        def _(e):
            for f in S.q["dve"]:
                f(e)

        @block.gpsimd
        def _(e):
            for f in S.q["pool"]:
                f(e)
    return nc


def make_rsw():
    r = np.zeros((128, 128), dtype=np.float32)
    for m in range(128):
        r[(m + 32) if m % 64 < 32 else (m - 32), m] = 1.0
    return r.astype(bf16)


def kernel(L0_attn_x, L0_attn_W_qkv, L0_attn_W_o, L0_ffn_weight):
    x = np.ascontiguousarray(np.asarray(L0_attn_x, dtype=np.float32))
    wqkv = np.ascontiguousarray(np.asarray(L0_attn_W_qkv, dtype=np.float32))
    wo = np.ascontiguousarray(np.asarray(L0_attn_W_o, dtype=np.float32))
    wft = np.ascontiguousarray(np.asarray(L0_ffn_weight, dtype=np.float32).T)
    rsw = make_rsw()

    nc = build("full")
    in_maps = [{"x": np.ascontiguousarray(x[c]), "wqkv": wqkv, "wo": wo,
                "wft": wft, "rsw": rsw} for c in range(8)]
    res = run_bass_kernel_spmd(nc, in_maps, core_ids=list(range(8)))
    return np.stack([r["y"] for r in res.results], axis=0)


# version 29
# speedup vs baseline: 1.0589x; 1.0131x over previous
import numpy as np
import ml_dtypes
import concourse.bass as bass
import concourse.mybir as mybir
from concourse.bass_utils import run_bass_kernel_spmd

F32 = mybir.dt.float32
BF16 = mybir.dt.bfloat16
I32 = mybir.dt.int32
AF = mybir.ActivationFunctionType
ALU = mybir.AluOpType
AX = mybir.AxisListType
bf16 = ml_dtypes.bfloat16

D = 1024
T = 2048
NH = 16
DH = 64
NC = 8
NT = 16
PI = float(np.pi)
EXP_C = 30.0

ENGS = ("pe", "act", "dve", "pool", "sync")
OWN = {"pe": "pe", "act": "act", "dve": "dve", "pool": "pool"}


class Sched:

    def __init__(self, nc):
        self.nc = nc
        self.q = {e: [] for e in ENGS}
        names = ["pe", "act", "dve", "pool", "dmas", "dmag", "out",
                 "x0", "x1", "x2", "x3"]
        self.sem = {k: nc.alloc_semaphore("s_" + k) for k in names}
        self.cnt = {k: 0 for k in names}
        self.watermark = {e: {k: 0 for k in names} for e in ENGS}

    def emit(self, eng, fn, waits=(), inc=None, amt=1):
        ws = []
        for s, v in waits:
            if v is None:
                continue
            assert v <= self.cnt[s], f"future token {s}={v} (now {self.cnt[s]})"
            if v > self.watermark[eng][s]:
                self.watermark[eng][s] = v
                ws.append((s, v))
        tok = None
        if inc is not None:
            self.cnt[inc] += amt
            tok = (inc, self.cnt[inc])
            if OWN.get(eng) == inc:
                self.watermark[eng][inc] = self.cnt[inc]
        sem = self.sem

        def run(e, fn=fn, ws=ws, inc=inc, amt=amt):
            for s, v in ws:
                e.wait_ge(sem[s], v)
            inst = fn(e)
            if inc is not None:
                inst.then_inc(sem[inc], amt)

        self.q[eng].append(run)
        return tok

    def tok(self, s):
        return (s, self.cnt[s])


def build(upto="full", taps=None):
    nc = bass.Bass("TRN2", target_bir_lowering=False)

    x = nc.dram_tensor("x", [T, D], F32, kind="ExternalInput")
    wqkv = nc.dram_tensor("wqkv", [D, 3 * D], F32, kind="ExternalInput")
    wo_d = nc.dram_tensor("wo", [D, D], F32, kind="ExternalInput")
    wft = nc.dram_tensor("wft", [D, D], F32, kind="ExternalInput")
    rsw_d = nc.dram_tensor("rsw", [128, 128], BF16, kind="ExternalInput")
    y = nc.dram_tensor("y", [T, D], F32, kind="ExternalOutput")

    xnT = nc.alloc_sbuf_tensor("xnT", [128, NC * T], BF16)
    qT = nc.alloc_sbuf_tensor("qT", [128, NC * T], BF16)
    kT = nc.alloc_sbuf_tensor("kT", [128, NC * T], BF16)
    vA = nc.alloc_sbuf_tensor("vA", [128, NT * NH * 65 + 64], BF16)
    wo8 = nc.alloc_sbuf_tensor("wo8", [128, 8192], BF16)
    swT = nc.alloc_sbuf_tensor("swT", [128, 8192], BF16)
    wd = nc.alloc_sbuf_tensor("wd", [128, 2048], BF16)
    qb = nc.alloc_sbuf_tensor("qb", [128, 1024], BF16)
    pt = nc.alloc_sbuf_tensor("pt", [128, 3072], BF16)
    rrow = nc.alloc_sbuf_tensor("rrow", [128, 2048], BF16)
    rrow32 = nc.alloc_sbuf_tensor("rrow32", [128, 512], F32)
    bcSB = nc.alloc_sbuf_tensor("bcSB", [64, 1024], BF16)
    ytt = nc.alloc_sbuf_tensor("ytt", [64, 1024], BF16)
    pool4 = nc.alloc_sbuf_tensor("pool4", [128, 4096], F32)
    poolb = nc.alloc_sbuf_tensor("poolb", [128, 4096], BF16)
    identf = nc.alloc_sbuf_tensor("identf", [128, 128], F32)
    rsw = nc.alloc_sbuf_tensor("rswb", [128, 128], BF16)
    onesb = nc.alloc_sbuf_tensor("onesb", [128, 64], BF16)
    biasC = nc.alloc_sbuf_tensor("biasC", [128, 4], F32)
    stats = nc.alloc_sbuf_tensor("stats", [128, 12], F32)
    mv = nc.alloc_sbuf_tensor("mv", [128, 2], F32)
    lncol = nc.alloc_sbuf_tensor("lncol", [128, 48], F32)
    ecol = nc.alloc_sbuf_tensor("ecol", [128, 1], F32)
    trow = swT.ap()[:, 0:4096].bitcast(F32)

    C4 = wo8[:, 0:2048]
    S4 = wo8[:, 2048:4096]

    def p4(i):
        return pool4[:, 1024 * i:1024 * (i + 1)]

    PSA = nc.alloc_psum_tensor("PSA", [128, 2048], F32)
    PSS = nc.alloc_psum_tensor("PSS", [128, 2048], F32)

    S = Sched(nc)
    env = {"xnT": xnT, "qT": qT, "kT": kT, "vA": vA, "wo8": wo8, "swT": swT,
           "pt": pt, "rrow": rrow, "pool4": pool4, "poolb": poolb, "trow": trow,
           "stats": stats, "mv": mv, "lncol": lncol, "qb": qb, "bcSB": bcSB,
           "ecol": ecol, "identf": identf, "onesb": onesb, "biasC": biasC}

    rsw_tok = S.emit("sync", lambda e: e.dma_start(out=rsw[:], in_=rsw_d.ap()),
                     inc="dmas", amt=16)

    S.emit("pool", lambda e: e.memset(identf[:], 1.0))
    id_tok = S.emit("pool", lambda e: e.affine_select(
        out=identf[:], in_=identf[:], pattern=[[-1, 128]],
        compare_op=ALU.is_equal, fill=0.0, channel_multiplier=1, base=0),
        inc="pool")
    ones_tok = S.emit("pool", lambda e: e.memset(onesb[:], 1.0), inc="pool")
    S.emit("pool", lambda e: e.memset(vA.ap()[:, NT * NH * 65:], 0.0))
    vones_tok = S.emit("pool", lambda e: e.memset(
        vA.ap()[:, 64:NT * NH * 65:65], 1.0), inc="pool")
    bias_tok = S.emit("dve", lambda e: e.memset(biasC[:, 0:1], -EXP_C), inc="dve")
    S.emit("dve", lambda e: e.memset(ecol[:], 1.0))
    S.emit("dve", lambda e: e.memset(ecol[0:32, :], -1.0))
    ecol_tok = S.emit("dve", lambda e: e.memset(ecol[64:96, :], -1.0), inc="dve")



    sqdump = xnT.ap()[:, 0:2048].bitcast(F32)
    red_toks = []
    xslot_tok = {}
    for tau in range(4):
        xslot_tok[tau % 4] = S.emit("sync", lambda e, tau=tau: e.dma_start(
            out=p4(tau), in_=x.ap()[128 * tau:128 * (tau + 1), :]),
            waits=[], inc="x%d" % (tau % 4), amt=16)
    redA = None
    for tau in range(NT):
        par = tau % 4
        xt = p4(par)
        S.emit("act", lambda e, xt=xt, tau=tau: e.activation(
            sqdump, xt, AF.Square, accum_out=lncol[:, 16 + tau:17 + tau]),
            waits=[xslot_tok[par]])
        redA = S.emit("dve", lambda e, xt=xt, tau=tau: e.reduce_sum(
            lncol[:, tau:tau + 1], xt, AX.X), waits=[xslot_tok[par], S.tok("act")],
            inc="dve")
        red_toks.append(redA)
        if tau + 4 < NT:
            xslot_tok[par] = S.emit("sync", lambda e, tau=tau, par=par: e.dma_start(
                out=p4(par), in_=x.ap()[128 * (tau + 4):128 * (tau + 5), :]),
                waits=[redA], inc="x%d" % par, amt=16)
    trow2 = swT.ap()[:, 4096:8192].bitcast(F32)
    it_tok = S.emit("pool", lambda e: e.iota(
        trow[0:1, :], [[1, 2048]], base=0, channel_multiplier=0,
        allow_small_or_imprecise_dtypes=True), inc="pool")
    S.emit("pool", lambda e: e.iota(
        trow2[0:1, 0:128].rearrange("a (r i) -> a r i", i=32),
        [[0, 4], [1, 32]], base=0, channel_multiplier=0,
        allow_small_or_imprecise_dtypes=True), inc="pool")
    S.emit("act", lambda e: e.activation(
        trow2[0:1, 0:128], trow2[0:1, 0:128], AF.Exp,
        scale=-float(np.log(10000.0)) / 32.0), waits=[S.tok("pool")])
    th_tok = S.emit("act", lambda e: e.drain(), inc="act")

    ang_tok = None
    for j in range(2):
        for s2 in range(2):
            cs = slice(1024 * j + 512 * s2, 1024 * j + 512 * s2 + 512)
            ang_tok = S.emit("pe", lambda e, cs=cs: e.matmul(
                PSA[:, cs], trow2[0:1, 0:128], trow[0:1, cs],
                start=True, stop=True), waits=[th_tok, it_tok], inc="pe")

    tgA = poolb.ap()[:, 0:2048].bitcast(F32)
    tgB = poolb.ap()[:, 2048:4096].bitcast(F32)
    sin_last = None
    for (dest, quarter, signed) in [(C4, True, False), (S4, False, True)]:
        for j in range(2):
            cs = slice(1024 * j, 1024 * (j + 1))
            w0 = [ang_tok] + ([sin_last] if sin_last else [])
            S.emit("dve", lambda e, cs=cs, quarter=quarter: e.tensor_scalar(
                tgA, PSA[:, cs], 1.0 / (2 * PI), 0.25 if quarter else 0.0,
                ALU.mult, ALU.add), waits=w0)
            S.emit("dve", lambda e: e.tensor_copy(tgB.bitcast(I32), tgA))
            S.emit("dve", lambda e: e.tensor_copy(tgA, tgB.bitcast(I32)))
            r_tok = S.emit("dve", lambda e, cs=cs: e.scalar_tensor_tensor(
                tgB, tgA, -2 * PI, PSA[:, cs], ALU.mult, ALU.add), inc="dve")
            if quarter:
                r_tok = S.emit("dve", lambda e: e.tensor_scalar_add(
                    tgB, tgB, PI / 2), inc="dve")
            if signed:
                s_tok = S.emit("act", lambda e: e.activation(
                    tgA, tgB, AF.Sin), waits=[r_tok], inc="act")
                sin_last = S.emit("dve", lambda e, dest=dest, cs=cs: e.tensor_scalar_mul(
                    dest[:, cs], tgA, ecol[:, 0:1]), waits=[s_tok, ecol_tok],
                    inc="dve")
            else:
                sin_last = S.emit("act", lambda e, dest=dest, cs=cs: e.activation(
                    dest[:, cs], tgB, AF.Sin), waits=[r_tok], inc="act")
    tables_tok = sin_last
    tables_dve = S.tok("dve")
    tables_act = S.tok("act")
    aA = S.emit("act", lambda e: e.drain(), inc="act")
    S.emit("dve", lambda e: e.drain(), waits=[aA])
    S.emit("dve", lambda e: e.tensor_scalar_mul(
        lncol[:, 0:16], lncol[:, 0:16], -1.0 / D))
    S.emit("dve", lambda e: e.tensor_scalar(
        lncol[:, 16:32], lncol[:, 16:32], 1.0 / D, 1e-5, ALU.mult, ALU.add))
    S.emit("dve", lambda e: e.drain())
    S.emit("dve", lambda e: e.tensor_tensor(
        lncol[:, 32:48], lncol[:, 0:16], lncol[:, 0:16], ALU.mult))
    S.emit("dve", lambda e: e.drain())
    S.emit("dve", lambda e: e.tensor_tensor(
        lncol[:, 16:32], lncol[:, 16:32], lncol[:, 32:48], ALU.subtract))
    vB = S.emit("dve", lambda e: e.drain(), inc="dve")
    S.emit("act", lambda e: e.activation(
        lncol[:, 32:48], lncol[:, 16:32], AF.Sqrt), waits=[vB])
    aB = S.emit("act", lambda e: e.drain(), inc="act")
    S.emit("dve", lambda e: e.reciprocal(lncol[:, 16:32], lncol[:, 32:48]),
           waits=[aB])
    S.emit("dve", lambda e: e.drain())
    S.emit("dve", lambda e: e.tensor_tensor(
        lncol[:, 32:48], lncol[:, 0:16], lncol[:, 16:32], ALU.mult))
    ln_tok = S.emit("dve", lambda e: e.drain(), inc="dve")

    evac_toks = [None, None]
    v_evac = []
    xnbufs = [pt.ap()[:, 0:2048].bitcast(F32),
              rrow.ap()[:, 0:2048].bitcast(F32)]
    wvfull = swT.ap()[:, 0:8192]
    wv_toks = [None, None]
    wv_toks[0] = S.emit("pool", lambda e: e.dma_start(
        out=wvfull.rearrange("p (c n) -> p c n", c=NC),
        in_=wqkv.ap()[:, 2048:3072].rearrange("(c p) n -> p c n", p=128)),
        waits=[tables_tok, tables_dve, tables_act], inc="dmag", amt=16)
    for tau in range(4):
        xslot_tok[tau % 4] = S.emit("sync", lambda e, tau=tau: e.dma_start(
            out=p4(tau), in_=x.ap()[128 * tau:128 * (tau + 1), :]),
            waits=[red_toks[min(tau + 12, NT - 1)]], inc="x%d" % (tau % 4),
            amt=16)
    tp_tok = None
    tp_prev = [None, None]
    vjobs = []
    for tau in range(NT):
        par = tau % 4
        xt = p4(par)
        xn = xnbufs[tau % 2]
        xw = [xslot_tok[par], ln_tok]
        if tp_prev[tau % 2] is not None:
            xw.append(tp_prev[tau % 2])
        xn_tok = S.emit("dve", lambda e, xt=xt, xn=xn, tau=tau: e.tensor_scalar(
            xn, xt, lncol[:, 16 + tau:17 + tau], lncol[:, 32 + tau:33 + tau],
            ALU.mult, ALU.add), waits=xw, inc="dve")
        if tau + 4 < NT:
            xslot_tok[par] = S.emit("sync", lambda e, tau=tau, par=par: e.dma_start(
                out=p4(par), in_=x.ap()[128 * (tau + 4):128 * (tau + 5), :]),
                waits=[xn_tok], inc="x%d" % par, amt=16)
        bpar = tau % 2
        pcols = 1024 * bpar
        for c in range(NC):
            w = [xn_tok, id_tok]
            if c == 0 and evac_toks[bpar] is not None:
                w.append(evac_toks[bpar])
            tp_tok = S.emit("pe", lambda e, xn=xn, c=c, pcols=pcols: e.transpose(
                PSS[:, pcols + 128 * c: pcols + 128 * (c + 1)],
                xn[:, 128 * c:128 * (c + 1)], identf[:]), waits=w,
                inc="pe" if c == NC - 1 else None) or tp_tok
        tp_prev[tau % 2] = tp_tok

        def mk_out(tau=tau):
            return xnT.ap().rearrange("p (c t) -> p c t", c=NC)[:, :, 128 * tau:128 * (tau + 1)]
        evac_toks[bpar] = S.emit("dve", lambda e, pcols=pcols, mo=mk_out: e.tensor_copy(
            mo(), PSS[:, pcols:pcols + 1024].rearrange("p (c r) -> p c r", c=NC)),
            waits=[tp_tok], inc="dve")

        def vjob(tau=tau, ev=evac_toks[bpar]):
            for vh in range(2):
                pcol = 512 * ((2 * tau + vh) % 4)
                mm_tok = None
                for c in range(NC):
                    w = []
                    if c == 0:
                        w = [S.tok("dmag"), ev]
                        if len(v_evac) >= 4:
                            w.append(v_evac[-4])
                    mm_tok = S.emit("pe", lambda e, c=c, tau=tau, pcol=pcol, vh=vh: e.matmul(
                        PSA[:, pcol:pcol + 512],
                        xnT.ap()[:, T * c + 128 * tau: T * c + 128 * (tau + 1)],
                        wvfull[:, 1024 * c + 512 * vh:1024 * c + 512 * vh + 512],
                        start=(c == 0), stop=(c == NC - 1)), waits=w,
                        inc="pe" if c == NC - 1 else None) or mm_tok

                def mk_vout(tau=tau, vh=vh):
                    a = vA.ap()[:, 0:NT * NH * 65].rearrange("p (t h s) -> p t h s", t=NT, h=NH)
                    return a[:, tau, 8 * vh:8 * vh + 8, 0:64]
                v_evac.append(S.emit("dve", lambda e, pcol=pcol, mo=mk_vout: e.tensor_copy(
                    mo(), PSA[:, pcol:pcol + 512].rearrange("p (h s) -> p h s", h=8)),
                    waits=[mm_tok, vones_tok], inc="dve"))
        vjobs.append(vjob)
        if tau >= 1:
            vjobs.pop(0)()
    while vjobs:
        vjobs.pop(0)()
    xnT_tok = evac_toks[(NT - 1) % 2]
    xnT_oth = evac_toks[NT % 2]

    if upto == "ph1":
        return finish(nc, S, y, taps, env)

    vA_tok = v_evac[-1]



    PR = PSS[:, 1024:2048]
    rope_toks = {}
    wd_tok = [None, None]
    wd_reader = [None, None]
    prep_state = {"pr_free": xnT_tok, "swap_live": None, "proj_last": None, "qb_ready": None}

    def make_prep(cc):
        dmas = []
        jobs = []
        for qk in range(2):
            kind = "q" if qk == 0 else "k"
            d = qk * 8 + cc
            buf = (2 * cc + qk) % 2
            dest = qT if qk == 0 else kT

            def dma_job(d=d, buf=buf):
                w = [wd_reader[buf]] if wd_reader[buf] is not None else []
                wd_tok[buf] = S.emit("pool", lambda e, d=d, buf=buf: e.dma_start(
                    out=wd[:, 1024 * buf:1024 * (buf + 1)]
                    .rearrange("p (c n) -> p c n", c=NC),
                    in_=wqkv.ap()[:, 128 * d:128 * (d + 1)]
                    .rearrange("(c p) n -> p c n", p=128)),
                    waits=w, inc="dmag", amt=16)
            dmas.append(dma_job)
            for j in range(2):
                for cg in range(4):
                    def proj_job(cg=cg, j=j, buf=buf):
                        for c in (2 * cg, 2 * cg + 1):
                            for s2 in range(2):
                                w = []
                                if c == 0 and s2 == 0:
                                    w = [S.tok("dmag"), xnT_tok, xnT_oth,
                                         prep_state["pr_free"]]
                                last = (c == NC - 1 and s2 == 1)
                                t = S.emit(
                                    "pe", lambda e, c=c, s2=s2, j=j, buf=buf: e.matmul(
                                        PR[:, 512 * s2:512 * (s2 + 1)],
                                        wd[:, 1024 * buf + 128 * c:
                                           1024 * buf + 128 * (c + 1)],
                                        xnT.ap()[:, T * c + 1024 * j + 512 * s2:
                                                 T * c + 1024 * j + 512 * (s2 + 1)],
                                        start=(c == 0), stop=(c == NC - 1)),
                                    waits=w, inc="pe" if last else None)
                                if last:
                                    wd_reader[buf] = t
                                    prep_state["proj_last"] = t
                    jobs.append(proj_job)

                def swap_job():
                    ev = S.emit("dve", lambda e: e.tensor_copy(
                        qb[:, 0:1024], PR[:, 0:1024]),
                        waits=[prep_state["proj_last"]], inc="dve")
                    prep_state["qb_ready"] = ev
                    S.emit("pe", lambda e: e.matmul(
                        PR[:, 0:512], rsw[:], qb[:, 0:512],
                        start=True, stop=True), waits=[ev, S.tok("dmas")])
                    prep_state["swap_live"] = S.emit("pe", lambda e: e.matmul(
                        PR[:, 512:1024], rsw[:], qb[:, 512:1024],
                        start=True, stop=True), inc="pe")
                jobs.append(swap_job)

                def rope_job(j=j, cc=cc, kind=kind, dest=dest):
                    sw2 = prep_state["swap_live"]
                    a_tok = S.emit("pool", lambda e, j=j: e.tensor_tensor(
                        poolb[:, 0:1024], qb[:, 0:1024],
                        C4[:, 1024 * j:1024 * (j + 1)], ALU.mult),
                        waits=[tables_tok, tables_dve, tables_act, vA_tok,
                               prep_state["qb_ready"]], inc="pool")
                    b_tok = S.emit("dve", lambda e, j=j: e.tensor_tensor(
                        poolb[:, 1024:2048], PR[:, 0:1024],
                        S4[:, 1024 * j:1024 * (j + 1)], ALU.mult),
                        waits=[sw2], inc="dve")
                    rt = S.emit("pool", lambda e, dest=dest, cc=cc, j=j:
                                e.tensor_tensor(
                                    dest[:, T * cc + 1024 * j:
                                         T * cc + 1024 * (j + 1)],
                                    poolb[:, 0:1024], poolb[:, 1024:2048],
                                    ALU.add), waits=[a_tok, b_tok], inc="pool")
                    rope_toks[(kind, cc, j)] = rt
                    prep_state["pr_free"] = b_tok
                jobs.append(rope_job)
        if cc == NC - 1:
            def wo_job():
                wo_holder["tok"] = S.emit("pool", lambda e: e.dma_start(
                    out=wo8.ap().rearrange("p (c n) -> p c n", c=NC),
                    in_=wo_d.ap().rearrange("(c p) n -> p c n", p=128)),
                    waits=[rope_toks[("k", NC - 1, 1)]], inc="dmag", amt=16)
            jobs.append(wo_job)
        return dmas, jobs

    def quarter_pieces():
        out = []
        for Q in range(4):
            for kt in range(min(16, 4 * Q + 4)):
                qs = max(512 * Q, 128 * kt)
                out.append((Q, kt, qs, 512 * (Q + 1)))
        return out

    PIECES = quarter_pieces()
    NP = len(PIECES)

    exp_tok = {}
    mask_tok = {}
    pv_tok = {}
    norm_tok = {}
    recip_tok = {}
    bcast_tok = {}
    for hh in range(2):
        for par in range(2):
            norm_tok[(hh, par)] = vA_tok
    pc_holder = [0]
    pending = []

    def pop_one():
        pending.pop(0)[2]()

    def run_chunk(cc, prep_dmas, prep_jobs):
        h_e, h_o = 2 * cc, 2 * cc + 1
        kw = [rope_toks[("k", cc, 0)], rope_toks[("k", cc, 1)],
              rope_toks[("q", cc, 0)], rope_toks[("q", cc, 1)], vA_tok]
        base = pc_holder[0]
        last_pv = {h_e: None, h_o: None}

        for i, (Q, kt, qs, qe) in enumerate(PIECES):
            p = base + i
            ln = qe - qs
            slot3 = p % 3
            par = Q % 2
            while (p - 3) >= 0 and (p - 3) not in pv_tok:
                assert pending, "pipeline stuck"
                pop_one()
            for hh in range(2):
                rb = 64 * hh
                w = list(kw)
                if (p - 1, hh) in exp_tok:
                    w.append(exp_tok[(p - 1, hh)])
                sm = S.emit("pe", lambda e, kt=kt, qs=qs, qe=qe, cc=cc, rb=rb, hh=hh: e.matmul(
                    PSS[:, 512 * hh:512 * hh + (qe - qs)],
                    kT.ap()[rb:rb + 64, T * cc + 128 * kt: T * cc + 128 * (kt + 1)],
                    qT.ap()[rb:rb + 64, T * cc + qs: T * cc + qe],
                    start=True, stop=True, tile_position=(rb, 0)),
                    waits=w, inc="pe")
                ew = [sm, bias_tok]
                if (p - 3, hh) in pv_tok:
                    ew.append(pv_tok[(p - 3, hh)])
                exp_tok[(p, hh)] = S.emit("act", lambda e, slot3=slot3, ln=ln, hh=hh: e.activation(
                    pt[:, 1024 * slot3 + 512 * hh: 1024 * slot3 + 512 * hh + ln],
                    PSS[:, 512 * hh:512 * hh + ln],
                    AF.Exp, bias=biasC[:, 0:1], scale=0.125), waits=ew, inc="act")
                if qs == 128 * kt:
                    mask_tok[(p, hh)] = S.emit("pool", lambda e, slot3=slot3, hh=hh: e.affine_select(
                        out=pt[:, 1024 * slot3 + 512 * hh:
                               1024 * slot3 + 512 * hh + 128],
                        in_=pt[:, 1024 * slot3 + 512 * hh:
                               1024 * slot3 + 512 * hh + 128],
                        pattern=[[1, 128]], compare_op=ALU.is_ge, fill=0.0,
                        channel_multiplier=-1, base=0),
                        waits=[exp_tok[(p, hh)]], inc="pool")

            def pv_job(p=p, Q=Q, kt=kt, qs=qs, qe=qe, slot3=slot3, par=par):
                stop = (kt == min(15, 4 * Q + 3))
                tok = None
                for hh, h in ((0, h_e), (1, h_o)):
                    w = [exp_tok[(p, hh)]]
                    if (p, hh) in mask_tok:
                        w.append(mask_tok[(p, hh)])
                    if kt == 0:
                        w.append(norm_tok[(hh, par)])
                    ycol = 1024 * hh + 512 * par
                    tok = S.emit("pe", lambda e, h=h, hh=hh, kt=kt, qs=qs, qe=qe,
                                 slot3=slot3, Q=Q, stop=stop, ycol=ycol: e.matmul(
                        PSA[0:128, ycol + qs - 512 * Q: ycol + qe - 512 * Q],
                        vA.ap()[:, (kt * NH + h) * 65:(kt * NH + h) * 65 + 128],
                        pt[:, 1024 * slot3 + 512 * hh:
                           1024 * slot3 + 512 * hh + (qe - qs)],
                        start=(kt == 0), stop=stop, skip_group_check=True),
                        waits=w, inc="pe")
                    last_pv[h] = tok
                    pv_tok[(p, hh)] = tok
                pv_tok[p] = tok
            pending.append(("pv", p, pv_job))


            if kt == min(15, 4 * Q + 3):
                for hh, h in ((0, h_e), (1, h_o)):
                    def recipB_job(hh=hh, h=h, par=par):
                        def _r(e):
                            with nc.allow_low_precision(
                                    reason="recip row feeds bf16 bcast"):
                                return e.reciprocal(
                                    rrow[64:65, 512 * (2 * hh + par):
                                         512 * (2 * hh + par) + 512],
                                    PSA[64:65, 1024 * hh + 512 * par:
                                        1024 * hh + 512 * par + 512])
                        recip_tok[(hh, par, 1)] = S.emit(
                            "dve", _r, waits=[last_pv[h]], inc="dve")
                    pending.append(("dve", None, recipB_job))
                for hh, h in ((0, h_e), (1, h_o)):
                    def bcastB_job(hh=hh, par=par):
                        bcast_tok[(hh, par, 1)] = S.emit(
                            "pe", lambda e, hh=hh, par=par: e.matmul(
                                PSA[64:128, 1024 * hh + 512 * par:
                                    1024 * hh + 512 * par + 512],
                                onesb[64:65, 0:64],
                                rrow[64:65, 512 * (2 * hh + par):
                                     512 * (2 * hh + par) + 512],
                                start=True, stop=True, skip_group_check=True),
                            waits=[recip_tok[(hh, par, 1)], ones_tok],
                            inc="pe")
                    pending.append(("pe", p + 7 + hh, bcastB_job))

                    def norm_job(hh=hh, Q=Q, cc=cc, par=par):
                        evb = S.emit("dve", lambda e, hh=hh, par=par: e.tensor_copy(
                            bcSB[:, 512 * hh:512 * (hh + 1)],
                            PSA[64:128, 1024 * hh + 512 * par:
                                1024 * hh + 512 * par + 512]),
                            waits=[bcast_tok[(hh, par, 1)]], inc="dve")
                        cs = slice(T * cc + 512 * Q, T * cc + 512 * (Q + 1))
                        if hh == 0:
                            nt = S.emit("dve", lambda e, hh=hh, par=par, cs=cs:
                                        e.tensor_tensor(
                                qT.ap()[0:64, cs],
                                PSA[0:64, 1024 * hh + 512 * par:
                                    1024 * hh + 512 * par + 512],
                                bcSB[:, 512 * hh:512 * (hh + 1)],
                                ALU.mult), waits=[evb], inc="dve")
                        else:
                            t1 = S.emit("dve", lambda e, hh=hh, par=par: e.tensor_tensor(
                                ytt[:, 0:512],
                                PSA[0:64, 1024 * hh + 512 * par:
                                    1024 * hh + 512 * par + 512],
                                bcSB[:, 512 * hh:512 * (hh + 1)],
                                ALU.mult), waits=[evb], inc="dve")
                            nt = S.emit("dve", lambda e, cs=cs: e.tensor_copy(
                                qT.ap()[64:128, cs], ytt[:, 0:512]),
                                waits=[t1], inc="dve")
                        norm_tok[(hh, par)] = nt
                    pending.append(("dve", None, norm_job))

            popped_pe = False
            while pending:
                tag, pp, _ = pending[0]
                if tag == "dve":
                    pop_one()
                elif tag == "pv" and pp <= p - 1:
                    pop_one()
                elif tag == "pe" and not popped_pe and (pp is None or pp <= p):
                    pop_one()
                    popped_pe = True
                else:
                    break
            if i < len(prep_dmas):
                prep_dmas[i]()
            elif i >= 6 and prep_jobs:
                prep_jobs.pop(0)()

        while prep_jobs:
            prep_jobs.pop(0)()
        pc_holder[0] = base + NP

    d0, j0 = make_prep(0)
    for f in d0 + j0:
        f()

    wo_holder = {}
    swh = {}

    def emit_sw():
        sw_toks = []
        for c in range(NC):
            par = c % 2
            ddw = [tables_tok, tables_act, tables_dve, S.tok("dve"), S.tok("pe")]
            if c >= 2:
                ddw.append(sw_toks[c - 2])
            S.emit("sync", lambda e, c=c, par=par: e.dma_start(
                out=p4(par), in_=wft.ap()[128 * c:128 * (c + 1), :]),
                waits=ddw, inc="dmas", amt=16)
            sw_toks.append(S.emit("act", lambda e, c=c, par=par: e.sign(
                swT[:, 1024 * c:1024 * (c + 1)], p4(par)), waits=[S.tok("dmas")],
                inc="act"))
        swh["tok"] = sw_toks[-1]

    for cc in range(NC):
        nd, nj = make_prep(cc + 1) if cc + 1 < NC else ([], [])
        run_chunk(cc, nd, nj)
        if cc == 0:
            emit_sw()
    while pending:
        pop_one()
    yT_tok = norm_tok[(1, 1)]
    yT_all = [norm_tok[k] for k in norm_tok]
    wo_tok = wo_holder["tok"]

    if upto == "ph3":
        return finish(nc, S, y, taps, env)

    regs = [PSA[:, 0:1024], PSA[:, 1024:2048], PSS[:, 0:1024], PSS[:, 1024:2048]]
    reg_free = [None, None, None, None]
    ph3_guard = yT_all + [S.tok("dve"), S.tok("act"), S.tok("pool")]
    swT_tok = swh["tok"]
    xr_tok = S.emit("sync", lambda e: e.dma_start(
        out=p4(1), in_=x.ap()[0:128, :]), waits=[swT_tok] + ph3_guard,
        inc="dmas", amt=16)
    g_free = None
    out_free = None
    for tau in range(NT):
        fr = (2 * tau) % 4
        wr = (2 * tau + 1) % 4
        ffn_tok = None
        for c in range(NC):
            for s2 in range(2):
                w = []
                if c == 0 and s2 == 0:
                    w = [swT_tok] + ph3_guard
                    if reg_free[fr] is not None:
                        w.append(reg_free[fr])
                ffn_tok = S.emit("pe", lambda e, c=c, s2=s2, fr=fr, tau=tau: e.matmul(
                    regs[fr][:, 512 * s2:512 * (s2 + 1)],
                    xnT.ap()[:, T * c + 128 * tau: T * c + 128 * (tau + 1)],
                    swT[:, 1024 * c + 512 * s2: 1024 * c + 512 * (s2 + 1)],
                    start=(c == 0), stop=(c == NC - 1)), waits=w,
                    inc="pe" if (c == NC - 1 and s2 == 1) else None) or ffn_tok
        wo_mm = None
        for c in range(NC):
            for s2 in range(2):
                w = []
                if c == 0 and s2 == 0:
                    w = [S.tok("dmag"), yT_tok]
                    if reg_free[wr] is not None:
                        w.append(reg_free[wr])
                wo_mm = S.emit("pe", lambda e, c=c, s2=s2, wr=wr, tau=tau: e.matmul(
                    regs[wr][:, 512 * s2:512 * (s2 + 1)],
                    qT.ap()[:, T * c + 128 * tau: T * c + 128 * (tau + 1)],
                    wo8[:, 1024 * c + 512 * s2: 1024 * c + 512 * (s2 + 1)],
                    start=(c == 0), stop=(c == NC - 1)), waits=w,
                    inc="pe" if (c == NC - 1 and s2 == 1) else None) or wo_mm
        gw = [ffn_tok]
        if g_free is not None:
            gw.append(g_free)
        g_tok = S.emit("act", lambda e, fr=fr: e.activation(
            p4(0), regs[fr][:], AF.Silu), waits=gw, inc="act")
        reg_free[fr] = g_tok
        t_tok = S.emit("dve", lambda e, wr=wr: e.tensor_tensor(
            p4(2), regs[wr][:], p4(0), ALU.mult), waits=[wo_mm, g_tok],
            inc="dve")
        reg_free[wr] = t_tok
        g_free = t_tok
        ow = [t_tok, S.tok("dmas")]
        if out_free is not None:
            ow.append(out_free)
        o_tok = S.emit("dve", lambda e: e.tensor_tensor(
            p4(3), p4(2), p4(1), ALU.add), waits=ow, inc="dve")
        if tau + 1 < NT:
            xr_tok = S.emit("sync", lambda e, tau=tau: e.dma_start(
                out=p4(1), in_=x.ap()[128 * (tau + 1):128 * (tau + 2), :]),
                waits=[o_tok], inc="dmas", amt=16)
        out_free = S.emit("sync", lambda e, tau=tau: e.dma_start(
            out=y.ap()[128 * tau:128 * (tau + 1), :], in_=p4(3)),
            waits=[o_tok], inc="out", amt=16)

    return finish(nc, S, y, taps, env)


def finish(nc, S, y, taps, env):
    if taps:
        for (name, shape, dt, get_ap) in taps:
            od = nc.dram_tensor(name, shape, dt, kind="ExternalOutput")
            waits = [(k, S.cnt[k]) for k in ("pe", "act", "dve", "pool")]
            S.emit("sync", lambda e, od=od, get_ap=get_ap: e.dma_start(
                out=od.ap(), in_=get_ap(env)), waits=waits, inc="out", amt=16)
    final_waits = [(k, S.cnt[k]) for k in S.cnt if S.cnt[k] > 0]
    sem = S.sem

    def fin(e):
        for s, v in final_waits:
            e.wait_ge(sem[s], v)
    S.q["sync"].append(fin)

    with nc.Block() as block:
        @block.sync
        def _(e):
            for f in S.q["sync"]:
                f(e)

        @block.tensor
        def _(e):
            for f in S.q["pe"]:
                f(e)

        @block.scalar
        def _(e):
            for f in S.q["act"]:
                f(e)

        @block.vector
        def _(e):
            for f in S.q["dve"]:
                f(e)

        @block.gpsimd
        def _(e):
            for f in S.q["pool"]:
                f(e)
    return nc


def make_rsw():
    r = np.zeros((128, 128), dtype=np.float32)
    for m in range(128):
        r[(m + 32) if m % 64 < 32 else (m - 32), m] = 1.0
    return r.astype(bf16)


def kernel(L0_attn_x, L0_attn_W_qkv, L0_attn_W_o, L0_ffn_weight):
    x = np.ascontiguousarray(np.asarray(L0_attn_x, dtype=np.float32))
    wqkv = np.ascontiguousarray(np.asarray(L0_attn_W_qkv, dtype=np.float32))
    wo = np.ascontiguousarray(np.asarray(L0_attn_W_o, dtype=np.float32))
    wft = np.ascontiguousarray(np.asarray(L0_ffn_weight, dtype=np.float32).T)
    rsw = make_rsw()

    nc = build("full")
    in_maps = [{"x": np.ascontiguousarray(x[c]), "wqkv": wqkv, "wo": wo,
                "wft": wft, "rsw": rsw} for c in range(8)]
    res = run_bass_kernel_spmd(nc, in_maps, core_ids=list(range(8)))
    return np.stack([r["y"] for r in res.results], axis=0)


# version 32
# speedup vs baseline: 1.0617x; 1.0026x over previous
import numpy as np
import ml_dtypes
import concourse.bass as bass
import concourse.mybir as mybir
from concourse.bass_utils import run_bass_kernel_spmd

F32 = mybir.dt.float32
BF16 = mybir.dt.bfloat16
I32 = mybir.dt.int32
AF = mybir.ActivationFunctionType
ALU = mybir.AluOpType
AX = mybir.AxisListType
bf16 = ml_dtypes.bfloat16

D = 1024
T = 2048
NH = 16
DH = 64
NC = 8
NT = 16
PI = float(np.pi)
EXP_C = 30.0

ENGS = ("pe", "act", "dve", "pool", "sync")
OWN = {"pe": "pe", "act": "act", "dve": "dve", "pool": "pool"}


class Sched:

    def __init__(self, nc):
        self.nc = nc
        self.q = {e: [] for e in ENGS}
        names = ["pe", "act", "dve", "pool", "dmas", "dmag", "out",
                 "x0", "x1", "x2", "x3"]
        self.sem = {k: nc.alloc_semaphore("s_" + k) for k in names}
        self.cnt = {k: 0 for k in names}
        self.watermark = {e: {k: 0 for k in names} for e in ENGS}

    def emit(self, eng, fn, waits=(), inc=None, amt=1):
        ws = []
        for s, v in waits:
            if v is None:
                continue
            assert v <= self.cnt[s], f"future token {s}={v} (now {self.cnt[s]})"
            if v > self.watermark[eng][s]:
                self.watermark[eng][s] = v
                ws.append((s, v))
        tok = None
        if inc is not None:
            self.cnt[inc] += amt
            tok = (inc, self.cnt[inc])
            if OWN.get(eng) == inc:
                self.watermark[eng][inc] = self.cnt[inc]
        sem = self.sem

        def run(e, fn=fn, ws=ws, inc=inc, amt=amt):
            for s, v in ws:
                e.wait_ge(sem[s], v)
            inst = fn(e)
            if inc is not None:
                inst.then_inc(sem[inc], amt)

        self.q[eng].append(run)
        return tok

    def tok(self, s):
        return (s, self.cnt[s])


def build(upto="full", taps=None):
    nc = bass.Bass("TRN2", target_bir_lowering=False)

    x = nc.dram_tensor("x", [T, D], F32, kind="ExternalInput")
    wqkv = nc.dram_tensor("wqkv", [D, 3 * D], F32, kind="ExternalInput")
    wo_d = nc.dram_tensor("wo", [D, D], F32, kind="ExternalInput")
    wft = nc.dram_tensor("wft", [D, D], F32, kind="ExternalInput")
    rsw_d = nc.dram_tensor("rsw", [128, 128], BF16, kind="ExternalInput")
    y = nc.dram_tensor("y", [T, D], F32, kind="ExternalOutput")

    xnT = nc.alloc_sbuf_tensor("xnT", [128, NC * T], BF16)
    qT = nc.alloc_sbuf_tensor("qT", [128, NC * T], BF16)
    kT = nc.alloc_sbuf_tensor("kT", [128, NC * T], BF16)
    vA = nc.alloc_sbuf_tensor("vA", [128, NT * NH * 65 + 64], BF16)
    wo8 = nc.alloc_sbuf_tensor("wo8", [128, 8192], BF16)
    swT = nc.alloc_sbuf_tensor("swT", [128, 8192], BF16)
    wd = nc.alloc_sbuf_tensor("wd", [128, 2048], BF16)
    qb = nc.alloc_sbuf_tensor("qb", [128, 1024], BF16)
    pt = nc.alloc_sbuf_tensor("pt", [128, 3072], BF16)
    rrow = nc.alloc_sbuf_tensor("rrow", [128, 2048], BF16)
    rrow32 = nc.alloc_sbuf_tensor("rrow32", [128, 512], F32)
    bcSB = nc.alloc_sbuf_tensor("bcSB", [64, 1024], BF16)
    ytt = nc.alloc_sbuf_tensor("ytt", [64, 1024], BF16)
    pool4 = nc.alloc_sbuf_tensor("pool4", [128, 4096], F32)
    poolb = nc.alloc_sbuf_tensor("poolb", [128, 4096], BF16)
    identf = nc.alloc_sbuf_tensor("identf", [128, 128], F32)
    rsw = nc.alloc_sbuf_tensor("rswb", [128, 128], BF16)
    onesb = nc.alloc_sbuf_tensor("onesb", [128, 64], BF16)
    biasC = nc.alloc_sbuf_tensor("biasC", [128, 4], F32)
    stats = nc.alloc_sbuf_tensor("stats", [128, 12], F32)
    mv = nc.alloc_sbuf_tensor("mv", [128, 2], F32)
    lncol = nc.alloc_sbuf_tensor("lncol", [128, 48], F32)
    ecol = nc.alloc_sbuf_tensor("ecol", [128, 1], F32)
    trow = swT.ap()[:, 0:4096].bitcast(F32)

    C4 = wo8[:, 0:2048]
    S4 = wo8[:, 2048:4096]

    def p4(i):
        return pool4[:, 1024 * i:1024 * (i + 1)]

    PSA = nc.alloc_psum_tensor("PSA", [128, 2048], F32)
    PSS = nc.alloc_psum_tensor("PSS", [128, 2048], F32)

    S = Sched(nc)
    env = {"xnT": xnT, "qT": qT, "kT": kT, "vA": vA, "wo8": wo8, "swT": swT,
           "pt": pt, "rrow": rrow, "pool4": pool4, "poolb": poolb, "trow": trow,
           "stats": stats, "mv": mv, "lncol": lncol, "qb": qb, "bcSB": bcSB,
           "ecol": ecol, "identf": identf, "onesb": onesb, "biasC": biasC}

    rsw_tok = S.emit("sync", lambda e: e.dma_start(out=rsw[:], in_=rsw_d.ap()),
                     inc="dmas", amt=16)

    S.emit("pool", lambda e: e.memset(identf[:], 1.0))
    id_tok = S.emit("pool", lambda e: e.affine_select(
        out=identf[:], in_=identf[:], pattern=[[-1, 128]],
        compare_op=ALU.is_equal, fill=0.0, channel_multiplier=1, base=0),
        inc="pool")
    ones_tok = S.emit("pool", lambda e: e.memset(onesb[:], 1.0), inc="pool")
    S.emit("pool", lambda e: e.memset(vA.ap()[:, NT * NH * 65:], 0.0))
    vones_tok = S.emit("pool", lambda e: e.memset(
        vA.ap()[:, 64:NT * NH * 65:65], 1.0), inc="pool")
    bias_tok = S.emit("dve", lambda e: e.memset(biasC[:, 0:1], -EXP_C), inc="dve")
    S.emit("dve", lambda e: e.memset(ecol[:], 1.0))
    S.emit("dve", lambda e: e.memset(ecol[0:32, :], -1.0))
    ecol_tok = S.emit("dve", lambda e: e.memset(ecol[64:96, :], -1.0), inc="dve")



    sqdump = xnT.ap()[:, 0:2048].bitcast(F32)
    red_toks = []
    xslot_tok = {}
    for tau in range(4):
        xslot_tok[tau % 4] = S.emit("sync", lambda e, tau=tau: e.dma_start(
            out=p4(tau), in_=x.ap()[128 * tau:128 * (tau + 1), :]),
            waits=[], inc="x%d" % (tau % 4), amt=16)
    redA = None
    for tau in range(NT):
        par = tau % 4
        xt = p4(par)
        S.emit("act", lambda e, xt=xt, tau=tau: e.activation(
            sqdump, xt, AF.Square, accum_out=lncol[:, 16 + tau:17 + tau]),
            waits=[xslot_tok[par]])
        redA = S.emit("dve", lambda e, xt=xt, tau=tau: e.reduce_sum(
            lncol[:, tau:tau + 1], xt, AX.X), waits=[xslot_tok[par], S.tok("act")],
            inc="dve")
        red_toks.append(redA)
        if tau + 4 < NT:
            xslot_tok[par] = S.emit("sync", lambda e, tau=tau, par=par: e.dma_start(
                out=p4(par), in_=x.ap()[128 * (tau + 4):128 * (tau + 5), :]),
                waits=[redA], inc="x%d" % par, amt=16)
    trow2 = swT.ap()[:, 4096:8192].bitcast(F32)
    it_tok = S.emit("pool", lambda e: e.iota(
        trow[0:1, :], [[1, 2048]], base=0, channel_multiplier=0,
        allow_small_or_imprecise_dtypes=True), inc="pool")
    S.emit("pool", lambda e: e.iota(
        trow2[0:1, 0:128].rearrange("a (r i) -> a r i", i=32),
        [[0, 4], [1, 32]], base=0, channel_multiplier=0,
        allow_small_or_imprecise_dtypes=True), inc="pool")
    S.emit("act", lambda e: e.activation(
        trow2[0:1, 0:128], trow2[0:1, 0:128], AF.Exp,
        scale=-float(np.log(10000.0)) / 32.0), waits=[S.tok("pool")])
    th_tok = S.emit("act", lambda e: e.drain(), inc="act")

    ang_tok = None
    for j in range(2):
        for s2 in range(2):
            cs = slice(1024 * j + 512 * s2, 1024 * j + 512 * s2 + 512)
            ang_tok = S.emit("pe", lambda e, cs=cs: e.matmul(
                PSA[:, cs], trow2[0:1, 0:128], trow[0:1, cs],
                start=True, stop=True), waits=[th_tok, it_tok], inc="pe")

    tgA = poolb.ap()[:, 0:2048].bitcast(F32)
    tgB = poolb.ap()[:, 2048:4096].bitcast(F32)
    sin_last = None
    for (dest, quarter, signed) in [(C4, True, False), (S4, False, True)]:
        for j in range(2):
            cs = slice(1024 * j, 1024 * (j + 1))
            w0 = [ang_tok] + ([sin_last] if sin_last else [])
            S.emit("dve", lambda e, cs=cs, quarter=quarter: e.tensor_scalar(
                tgA, PSA[:, cs], 1.0 / (2 * PI), 0.25 if quarter else 0.0,
                ALU.mult, ALU.add), waits=w0)
            S.emit("dve", lambda e: e.tensor_copy(tgB.bitcast(I32), tgA))
            S.emit("dve", lambda e: e.tensor_copy(tgA, tgB.bitcast(I32)))
            r_tok = S.emit("dve", lambda e, cs=cs: e.scalar_tensor_tensor(
                tgB, tgA, -2 * PI, PSA[:, cs], ALU.mult, ALU.add), inc="dve")
            if quarter:
                r_tok = S.emit("dve", lambda e: e.tensor_scalar_add(
                    tgB, tgB, PI / 2), inc="dve")
            if signed:
                s_tok = S.emit("act", lambda e: e.activation(
                    tgA, tgB, AF.Sin), waits=[r_tok], inc="act")
                sin_last = S.emit("dve", lambda e, dest=dest, cs=cs: e.tensor_scalar_mul(
                    dest[:, cs], tgA, ecol[:, 0:1]), waits=[s_tok, ecol_tok],
                    inc="dve")
            else:
                sin_last = S.emit("act", lambda e, dest=dest, cs=cs: e.activation(
                    dest[:, cs], tgB, AF.Sin), waits=[r_tok], inc="act")
    tables_tok = sin_last
    tables_dve = S.tok("dve")
    tables_act = S.tok("act")
    aA = S.emit("act", lambda e: e.drain(), inc="act")
    S.emit("dve", lambda e: e.drain(), waits=[aA])
    S.emit("dve", lambda e: e.tensor_scalar_mul(
        lncol[:, 0:16], lncol[:, 0:16], -1.0 / D))
    S.emit("dve", lambda e: e.tensor_scalar(
        lncol[:, 16:32], lncol[:, 16:32], 1.0 / D, 1e-5, ALU.mult, ALU.add))
    S.emit("dve", lambda e: e.drain())
    S.emit("dve", lambda e: e.tensor_tensor(
        lncol[:, 32:48], lncol[:, 0:16], lncol[:, 0:16], ALU.mult))
    S.emit("dve", lambda e: e.drain())
    S.emit("dve", lambda e: e.tensor_tensor(
        lncol[:, 16:32], lncol[:, 16:32], lncol[:, 32:48], ALU.subtract))
    vB = S.emit("dve", lambda e: e.drain(), inc="dve")
    S.emit("act", lambda e: e.activation(
        lncol[:, 32:48], lncol[:, 16:32], AF.Sqrt), waits=[vB])
    aB = S.emit("act", lambda e: e.drain(), inc="act")
    S.emit("dve", lambda e: e.reciprocal(lncol[:, 16:32], lncol[:, 32:48]),
           waits=[aB])
    S.emit("dve", lambda e: e.drain())
    S.emit("dve", lambda e: e.tensor_tensor(
        lncol[:, 32:48], lncol[:, 0:16], lncol[:, 16:32], ALU.mult))
    ln_tok = S.emit("dve", lambda e: e.drain(), inc="dve")

    evac_toks = [None, None]
    v_evac = []
    xnbufs = [pt.ap()[:, 0:2048].bitcast(F32),
              rrow.ap()[:, 0:2048].bitcast(F32)]
    wvfull = swT.ap()[:, 0:8192]
    wv_toks = [None, None]
    wv_toks[0] = S.emit("pool", lambda e: e.dma_start(
        out=wvfull.rearrange("p (c n) -> p c n", c=NC),
        in_=wqkv.ap()[:, 2048:3072].rearrange("(c p) n -> p c n", p=128)),
        waits=[tables_tok, tables_dve, tables_act], inc="dmag", amt=16)
    for tau in range(4):
        xslot_tok[tau % 4] = S.emit("sync", lambda e, tau=tau: e.dma_start(
            out=p4(tau), in_=x.ap()[128 * tau:128 * (tau + 1), :]),
            waits=[red_toks[min(tau + 12, NT - 1)]], inc="x%d" % (tau % 4),
            amt=16)
    tp_tok = None
    tp_prev = [None, None]
    vjobs = []
    for tau in range(NT):
        par = tau % 4
        xt = p4(par)
        xn = xnbufs[tau % 2]
        xw = [xslot_tok[par], ln_tok]
        if tp_prev[tau % 2] is not None:
            xw.append(tp_prev[tau % 2])
        xn_tok = S.emit("dve", lambda e, xt=xt, xn=xn, tau=tau: e.tensor_scalar(
            xn, xt, lncol[:, 16 + tau:17 + tau], lncol[:, 32 + tau:33 + tau],
            ALU.mult, ALU.add), waits=xw, inc="dve")
        if tau + 4 < NT:
            xslot_tok[par] = S.emit("sync", lambda e, tau=tau, par=par: e.dma_start(
                out=p4(par), in_=x.ap()[128 * (tau + 4):128 * (tau + 5), :]),
                waits=[xn_tok], inc="x%d" % par, amt=16)
        bpar = tau % 2
        pcols = 1024 * bpar
        for c in range(NC):
            w = [xn_tok, id_tok]
            if c == 0 and evac_toks[bpar] is not None:
                w.append(evac_toks[bpar])
            tp_tok = S.emit("pe", lambda e, xn=xn, c=c, pcols=pcols: e.transpose(
                PSS[:, pcols + 128 * c: pcols + 128 * (c + 1)],
                xn[:, 128 * c:128 * (c + 1)], identf[:]), waits=w,
                inc="pe" if c == NC - 1 else None) or tp_tok
        tp_prev[tau % 2] = tp_tok

        def mk_out(tau=tau):
            return xnT.ap().rearrange("p (c t) -> p c t", c=NC)[:, :, 128 * tau:128 * (tau + 1)]
        evac_toks[bpar] = S.emit("dve", lambda e, pcols=pcols, mo=mk_out: e.tensor_copy(
            mo(), PSS[:, pcols:pcols + 1024].rearrange("p (c r) -> p c r", c=NC)),
            waits=[tp_tok], inc="dve")

        def vjob(tau=tau, ev=evac_toks[bpar]):
            for vh in range(2):
                pcol = 512 * ((2 * tau + vh) % 4)
                mm_tok = None
                for c in range(NC):
                    w = []
                    if c == 0:
                        w = [S.tok("dmag"), ev]
                        if len(v_evac) >= 4:
                            w.append(v_evac[-4])
                    mm_tok = S.emit("pe", lambda e, c=c, tau=tau, pcol=pcol, vh=vh: e.matmul(
                        PSA[:, pcol:pcol + 512],
                        xnT.ap()[:, T * c + 128 * tau: T * c + 128 * (tau + 1)],
                        wvfull[:, 1024 * c + 512 * vh:1024 * c + 512 * vh + 512],
                        start=(c == 0), stop=(c == NC - 1)), waits=w,
                        inc="pe" if c == NC - 1 else None) or mm_tok

                def mk_vout(tau=tau, vh=vh):
                    a = vA.ap()[:, 0:NT * NH * 65].rearrange("p (t h s) -> p t h s", t=NT, h=NH)
                    return a[:, tau, 8 * vh:8 * vh + 8, 0:64]
                v_evac.append(S.emit("dve", lambda e, pcol=pcol, mo=mk_vout: e.tensor_copy(
                    mo(), PSA[:, pcol:pcol + 512].rearrange("p (h s) -> p h s", h=8)),
                    waits=[mm_tok, vones_tok], inc="dve"))
        vjobs.append(vjob)
        if tau >= 1:
            vjobs.pop(0)()
    while vjobs:
        vjobs.pop(0)()
    xnT_tok = evac_toks[(NT - 1) % 2]
    xnT_oth = evac_toks[NT % 2]

    if upto == "ph1":
        return finish(nc, S, y, taps, env)

    vA_tok = v_evac[-1]



    PR = PSS[:, 1024:2048]
    rope_toks = {}
    wd_tok = [None, None]
    wd_reader = [None, None]
    prep_state = {"pr_free": xnT_tok, "swap_live": None, "proj_last": None, "qb_ready": None}

    def make_prep(cc):
        dmas = []
        jobs = []
        for qk in range(2):
            kind = "q" if qk == 0 else "k"
            d = qk * 8 + cc
            buf = (2 * cc + qk) % 2
            dest = qT if qk == 0 else kT

            def dma_job(d=d, buf=buf):
                w = [wd_reader[buf]] if wd_reader[buf] is not None else []
                wd_tok[buf] = S.emit("pool", lambda e, d=d, buf=buf: e.dma_start(
                    out=wd[:, 1024 * buf:1024 * (buf + 1)]
                    .rearrange("p (c n) -> p c n", c=NC),
                    in_=wqkv.ap()[:, 128 * d:128 * (d + 1)]
                    .rearrange("(c p) n -> p c n", p=128)),
                    waits=w, inc="dmag", amt=16)
            dmas.append(dma_job)
            for j in range(2):
                for cg in range(4):
                    def proj_job(cg=cg, j=j, buf=buf):
                        for c in (2 * cg, 2 * cg + 1):
                            for s2 in range(2):
                                w = []
                                if c == 0 and s2 == 0:
                                    w = [S.tok("dmag"), xnT_tok, xnT_oth,
                                         prep_state["pr_free"]]
                                last = (c == NC - 1 and s2 == 1)
                                t = S.emit(
                                    "pe", lambda e, c=c, s2=s2, j=j, buf=buf: e.matmul(
                                        PR[:, 512 * s2:512 * (s2 + 1)],
                                        wd[:, 1024 * buf + 128 * c:
                                           1024 * buf + 128 * (c + 1)],
                                        xnT.ap()[:, T * c + 1024 * j + 512 * s2:
                                                 T * c + 1024 * j + 512 * (s2 + 1)],
                                        start=(c == 0), stop=(c == NC - 1)),
                                    waits=w, inc="pe" if last else None)
                                if last:
                                    wd_reader[buf] = t
                                    prep_state["proj_last"] = t
                    jobs.append(proj_job)

                def swap_job():
                    ev = S.emit("dve", lambda e: e.tensor_copy(
                        qb[:, 0:1024], PR[:, 0:1024]),
                        waits=[prep_state["proj_last"]], inc="dve")
                    prep_state["qb_ready"] = ev
                    S.emit("pe", lambda e: e.matmul(
                        PR[:, 0:512], rsw[:], qb[:, 0:512],
                        start=True, stop=True), waits=[ev, rsw_tok])
                    prep_state["swap_live"] = S.emit("pe", lambda e: e.matmul(
                        PR[:, 512:1024], rsw[:], qb[:, 512:1024],
                        start=True, stop=True), inc="pe")
                jobs.append(swap_job)

                def rope_job(j=j, cc=cc, kind=kind, dest=dest):
                    sw2 = prep_state["swap_live"]
                    a_tok = S.emit("pool", lambda e, j=j: e.tensor_tensor(
                        poolb[:, 0:1024], qb[:, 0:1024],
                        C4[:, 1024 * j:1024 * (j + 1)], ALU.mult),
                        waits=[tables_tok, tables_dve, tables_act, vA_tok,
                               prep_state["qb_ready"]], inc="pool")
                    b_tok = S.emit("dve", lambda e, j=j: e.tensor_tensor(
                        poolb[:, 1024:2048], PR[:, 0:1024],
                        S4[:, 1024 * j:1024 * (j + 1)], ALU.mult),
                        waits=[sw2], inc="dve")
                    rt = S.emit("pool", lambda e, dest=dest, cc=cc, j=j:
                                e.tensor_tensor(
                                    dest[:, T * cc + 1024 * j:
                                         T * cc + 1024 * (j + 1)],
                                    poolb[:, 0:1024], poolb[:, 1024:2048],
                                    ALU.add), waits=[a_tok, b_tok], inc="pool")
                    rope_toks[(kind, cc, j)] = rt
                    prep_state["pr_free"] = b_tok
                jobs.append(rope_job)
        if cc == NC - 1:
            def wo_job():
                wo_holder["tok"] = S.emit("pool", lambda e: e.dma_start(
                    out=wo8.ap().rearrange("p (c n) -> p c n", c=NC),
                    in_=wo_d.ap().rearrange("(c p) n -> p c n", p=128)),
                    waits=[rope_toks[("k", NC - 1, 1)]], inc="dmag", amt=16)
            jobs.append(wo_job)
        return dmas, jobs

    def quarter_pieces():
        out = []
        for Q in range(4):
            for kt in range(min(16, 4 * Q + 4)):
                qs = max(512 * Q, 128 * kt)
                out.append((Q, kt, qs, 512 * (Q + 1)))
        return out

    PIECES = quarter_pieces()
    NP = len(PIECES)

    exp_tok = {}
    mask_tok = {}
    pv_tok = {}
    norm_tok = {}
    recip_tok = {}
    bcast_tok = {}
    for hh in range(2):
        for par in range(2):
            norm_tok[(hh, par)] = vA_tok
    pc_holder = [0]
    pending = []

    def pop_one():
        pending.pop(0)[2]()

    def run_chunk(cc, prep_dmas, prep_jobs):
        h_e, h_o = 2 * cc, 2 * cc + 1
        kw = [rope_toks[("k", cc, 0)], rope_toks[("k", cc, 1)],
              rope_toks[("q", cc, 0)], rope_toks[("q", cc, 1)], vA_tok]
        base = pc_holder[0]
        last_pv = {h_e: None, h_o: None}

        for i, (Q, kt, qs, qe) in enumerate(PIECES):
            p = base + i
            ln = qe - qs
            slot3 = p % 3
            par = Q % 2
            while (p - 3) >= 0 and (p - 3) not in pv_tok:
                assert pending, "pipeline stuck"
                pop_one()
            for hh in range(2):
                rb = 64 * hh
                w = list(kw)
                if (p - 1, hh) in exp_tok:
                    w.append(exp_tok[(p - 1, hh)])
                sm = S.emit("pe", lambda e, kt=kt, qs=qs, qe=qe, cc=cc, rb=rb, hh=hh: e.matmul(
                    PSS[:, 512 * hh:512 * hh + (qe - qs)],
                    kT.ap()[rb:rb + 64, T * cc + 128 * kt: T * cc + 128 * (kt + 1)],
                    qT.ap()[rb:rb + 64, T * cc + qs: T * cc + qe],
                    start=True, stop=True, tile_position=(rb, 0)),
                    waits=w, inc="pe")
                ew = [sm, bias_tok]
                if (p - 3, hh) in pv_tok:
                    ew.append(pv_tok[(p - 3, hh)])
                exp_tok[(p, hh)] = S.emit("act", lambda e, slot3=slot3, ln=ln, hh=hh: e.activation(
                    pt[:, 1024 * slot3 + 512 * hh: 1024 * slot3 + 512 * hh + ln],
                    PSS[:, 512 * hh:512 * hh + ln],
                    AF.Exp, bias=biasC[:, 0:1], scale=0.125), waits=ew, inc="act")
                if qs == 128 * kt:
                    mask_tok[(p, hh)] = S.emit("pool", lambda e, slot3=slot3, hh=hh: e.affine_select(
                        out=pt[:, 1024 * slot3 + 512 * hh:
                               1024 * slot3 + 512 * hh + 128],
                        in_=pt[:, 1024 * slot3 + 512 * hh:
                               1024 * slot3 + 512 * hh + 128],
                        pattern=[[1, 128]], compare_op=ALU.is_ge, fill=0.0,
                        channel_multiplier=-1, base=0),
                        waits=[exp_tok[(p, hh)]], inc="pool")

            def pv_job(p=p, Q=Q, kt=kt, qs=qs, qe=qe, slot3=slot3, par=par):
                stop = (kt == min(15, 4 * Q + 3))
                tok = None
                for hh, h in ((0, h_e), (1, h_o)):
                    w = [exp_tok[(p, hh)]]
                    if (p, hh) in mask_tok:
                        w.append(mask_tok[(p, hh)])
                    if kt == 0:
                        w.append(norm_tok[(hh, par)])
                    ycol = 1024 * hh + 512 * par
                    tok = S.emit("pe", lambda e, h=h, hh=hh, kt=kt, qs=qs, qe=qe,
                                 slot3=slot3, Q=Q, stop=stop, ycol=ycol: e.matmul(
                        PSA[0:128, ycol + qs - 512 * Q: ycol + qe - 512 * Q],
                        vA.ap()[:, (kt * NH + h) * 65:(kt * NH + h) * 65 + 128],
                        pt[:, 1024 * slot3 + 512 * hh:
                           1024 * slot3 + 512 * hh + (qe - qs)],
                        start=(kt == 0), stop=stop, skip_group_check=True),
                        waits=w, inc="pe")
                    last_pv[h] = tok
                    pv_tok[(p, hh)] = tok
                pv_tok[p] = tok
            pending.append(("pv", p, pv_job))


            if kt == min(15, 4 * Q + 3):
                for hh, h in ((0, h_e), (1, h_o)):
                    def recipB_job(hh=hh, h=h, par=par):
                        def _r(e):
                            with nc.allow_low_precision(
                                    reason="recip row feeds bf16 bcast"):
                                return e.reciprocal(
                                    rrow[64:65, 512 * (2 * hh + par):
                                         512 * (2 * hh + par) + 512],
                                    PSA[64:65, 1024 * hh + 512 * par:
                                        1024 * hh + 512 * par + 512])
                        recip_tok[(hh, par, 1)] = S.emit(
                            "dve", _r, waits=[last_pv[h]], inc="dve")
                    pending.append(("dve", None, recipB_job))
                for hh, h in ((0, h_e), (1, h_o)):
                    def bcastB_job(hh=hh, par=par):
                        bcast_tok[(hh, par, 1)] = S.emit(
                            "pe", lambda e, hh=hh, par=par: e.matmul(
                                PSA[64:128, 1024 * hh + 512 * par:
                                    1024 * hh + 512 * par + 512],
                                onesb[64:65, 0:64],
                                rrow[64:65, 512 * (2 * hh + par):
                                     512 * (2 * hh + par) + 512],
                                start=True, stop=True, skip_group_check=True),
                            waits=[recip_tok[(hh, par, 1)], ones_tok],
                            inc="pe")
                    pending.append(("pe", p + 7 + hh, bcastB_job))

                    def norm_job(hh=hh, Q=Q, cc=cc, par=par):
                        evb = S.emit("dve", lambda e, hh=hh, par=par: e.tensor_copy(
                            bcSB[:, 512 * hh:512 * (hh + 1)],
                            PSA[64:128, 1024 * hh + 512 * par:
                                1024 * hh + 512 * par + 512]),
                            waits=[bcast_tok[(hh, par, 1)]], inc="dve")
                        cs = slice(T * cc + 512 * Q, T * cc + 512 * (Q + 1))
                        if hh == 0:
                            nt = S.emit("dve", lambda e, hh=hh, par=par, cs=cs:
                                        e.tensor_tensor(
                                qT.ap()[0:64, cs],
                                PSA[0:64, 1024 * hh + 512 * par:
                                    1024 * hh + 512 * par + 512],
                                bcSB[:, 512 * hh:512 * (hh + 1)],
                                ALU.mult), waits=[evb], inc="dve")
                        else:
                            t1 = S.emit("dve", lambda e, hh=hh, par=par: e.tensor_tensor(
                                ytt[:, 0:512],
                                PSA[0:64, 1024 * hh + 512 * par:
                                    1024 * hh + 512 * par + 512],
                                bcSB[:, 512 * hh:512 * (hh + 1)],
                                ALU.mult), waits=[evb], inc="dve")
                            nt = S.emit("dve", lambda e, cs=cs: e.tensor_copy(
                                qT.ap()[64:128, cs], ytt[:, 0:512]),
                                waits=[t1], inc="dve")
                        norm_tok[(hh, par)] = nt
                    pending.append(("dve", None, norm_job))

            popped_pe = False
            while pending:
                tag, pp, _ = pending[0]
                if tag == "dve":
                    pop_one()
                elif tag == "pv" and pp <= p - 1:
                    pop_one()
                elif tag == "pe" and not popped_pe and (pp is None or pp <= p):
                    pop_one()
                    popped_pe = True
                else:
                    break
            if i < len(prep_dmas):
                prep_dmas[i]()
            elif i >= 6 and prep_jobs:
                prep_jobs.pop(0)()

        while prep_jobs:
            prep_jobs.pop(0)()
        pc_holder[0] = base + NP

    d0, j0 = make_prep(0)
    for f in d0 + j0:
        f()

    wo_holder = {}
    swh = {}

    def emit_sw():
        sw_toks = []
        for c in range(NC):
            par = c % 2
            ddw = [tables_tok, tables_act, tables_dve, S.tok("dve"), S.tok("pe")]
            if c >= 2:
                ddw.append(sw_toks[c - 2])
            S.emit("sync", lambda e, c=c, par=par: e.dma_start(
                out=p4(par), in_=wft.ap()[128 * c:128 * (c + 1), :]),
                waits=ddw, inc="dmas", amt=16)
            sw_toks.append(S.emit("act", lambda e, c=c, par=par: e.sign(
                swT[:, 1024 * c:1024 * (c + 1)], p4(par)), waits=[S.tok("dmas")],
                inc="act"))
        swh["tok"] = sw_toks[-1]

    for cc in range(NC):
        nd, nj = make_prep(cc + 1) if cc + 1 < NC else ([], [])
        run_chunk(cc, nd, nj)
        if cc == 0:
            emit_sw()
    while pending:
        pop_one()
    yT_tok = norm_tok[(1, 1)]
    yT_all = [norm_tok[k] for k in norm_tok]
    wo_tok = wo_holder["tok"]

    if upto == "ph3":
        return finish(nc, S, y, taps, env)

    regs = [PSA[:, 0:1024], PSA[:, 1024:2048], PSS[:, 0:1024], PSS[:, 1024:2048]]
    reg_free = [None, None, None, None]
    ph3_guard = yT_all + [S.tok("dve"), S.tok("act"), S.tok("pool")]
    swT_tok = swh["tok"]
    wo_dmag = S.tok("dmag")
    xrv = [poolb.ap()[:, 0:2048].bitcast(F32),
           poolb.ap()[:, 2048:4096].bitcast(F32)]
    ostg = [p4(3), p4(1)]
    xr_toks = []
    for i in range(2):
        xr_toks.append(S.emit("pool", lambda e, i=i: e.dma_start(
            out=xrv[i], in_=x.ap()[128 * i:128 * (i + 1), :]),
            waits=(ph3_guard if i == 0 else []), inc="dmag", amt=16))
    g_free = None
    out_free = [None, None]
    for tau in range(NT):
        fr = (2 * tau) % 4
        wr = (2 * tau + 1) % 4
        ffn_tok = None
        for c in range(NC):
            for s2 in range(2):
                w = []
                if c == 0 and s2 == 0:
                    w = [swT_tok] + ph3_guard
                    if reg_free[fr] is not None:
                        w.append(reg_free[fr])
                ffn_tok = S.emit("pe", lambda e, c=c, s2=s2, fr=fr, tau=tau: e.matmul(
                    regs[fr][:, 512 * s2:512 * (s2 + 1)],
                    xnT.ap()[:, T * c + 128 * tau: T * c + 128 * (tau + 1)],
                    swT[:, 1024 * c + 512 * s2: 1024 * c + 512 * (s2 + 1)],
                    start=(c == 0), stop=(c == NC - 1)), waits=w,
                    inc="pe" if (c == NC - 1 and s2 == 1) else None) or ffn_tok
        wo_mm = None
        for c in range(NC):
            for s2 in range(2):
                w = []
                if c == 0 and s2 == 0:
                    w = [wo_dmag, yT_tok]
                    if reg_free[wr] is not None:
                        w.append(reg_free[wr])
                wo_mm = S.emit("pe", lambda e, c=c, s2=s2, wr=wr, tau=tau: e.matmul(
                    regs[wr][:, 512 * s2:512 * (s2 + 1)],
                    qT.ap()[:, T * c + 128 * tau: T * c + 128 * (tau + 1)],
                    wo8[:, 1024 * c + 512 * s2: 1024 * c + 512 * (s2 + 1)],
                    start=(c == 0), stop=(c == NC - 1)), waits=w,
                    inc="pe" if (c == NC - 1 and s2 == 1) else None) or wo_mm
        gw = [ffn_tok]
        if g_free is not None:
            gw.append(g_free)
        g_tok = S.emit("act", lambda e, fr=fr: e.activation(
            p4(0), regs[fr][:], AF.Silu), waits=gw, inc="act")
        reg_free[fr] = g_tok
        t_tok = S.emit("dve", lambda e, wr=wr: e.tensor_tensor(
            p4(2), regs[wr][:], p4(0), ALU.mult), waits=[wo_mm, g_tok],
            inc="dve")
        reg_free[wr] = t_tok
        g_free = t_tok
        ow = [t_tok, xr_toks[tau]]
        if out_free[tau % 2] is not None:
            ow.append(out_free[tau % 2])
        o_tok = S.emit("dve", lambda e, tau=tau: e.tensor_tensor(
            ostg[tau % 2], p4(2), xrv[tau % 2], ALU.add), waits=ow,
            inc="dve")
        if tau + 2 < NT:
            xr_toks.append(S.emit("pool", lambda e, tau=tau: e.dma_start(
                out=xrv[tau % 2],
                in_=x.ap()[128 * (tau + 2):128 * (tau + 3), :]),
                waits=[o_tok], inc="dmag", amt=16))
        out_free[tau % 2] = S.emit("sync", lambda e, tau=tau: e.dma_start(
            out=y.ap()[128 * tau:128 * (tau + 1), :], in_=ostg[tau % 2]),
            waits=[o_tok], inc="out", amt=16)

    return finish(nc, S, y, taps, env)


def finish(nc, S, y, taps, env):
    if taps:
        for (name, shape, dt, get_ap) in taps:
            od = nc.dram_tensor(name, shape, dt, kind="ExternalOutput")
            waits = [(k, S.cnt[k]) for k in ("pe", "act", "dve", "pool")]
            S.emit("sync", lambda e, od=od, get_ap=get_ap: e.dma_start(
                out=od.ap(), in_=get_ap(env)), waits=waits, inc="out", amt=16)
    final_waits = [(k, S.cnt[k]) for k in S.cnt if S.cnt[k] > 0]
    sem = S.sem

    def fin(e):
        for s, v in final_waits:
            e.wait_ge(sem[s], v)
    S.q["sync"].append(fin)

    with nc.Block() as block:
        @block.sync
        def _(e):
            for f in S.q["sync"]:
                f(e)

        @block.tensor
        def _(e):
            for f in S.q["pe"]:
                f(e)

        @block.scalar
        def _(e):
            for f in S.q["act"]:
                f(e)

        @block.vector
        def _(e):
            for f in S.q["dve"]:
                f(e)

        @block.gpsimd
        def _(e):
            for f in S.q["pool"]:
                f(e)
    return nc


def make_rsw():
    r = np.zeros((128, 128), dtype=np.float32)
    for m in range(128):
        r[(m + 32) if m % 64 < 32 else (m - 32), m] = 1.0
    return r.astype(bf16)


def kernel(L0_attn_x, L0_attn_W_qkv, L0_attn_W_o, L0_ffn_weight):
    x = np.ascontiguousarray(np.asarray(L0_attn_x, dtype=np.float32))
    wqkv = np.ascontiguousarray(np.asarray(L0_attn_W_qkv, dtype=np.float32))
    wo = np.ascontiguousarray(np.asarray(L0_attn_W_o, dtype=np.float32))
    wft = np.ascontiguousarray(np.asarray(L0_ffn_weight, dtype=np.float32).T)
    rsw = make_rsw()

    nc = build("full")
    in_maps = [{"x": np.ascontiguousarray(x[c]), "wqkv": wqkv, "wo": wo,
                "wft": wft, "rsw": rsw} for c in range(8)]
    res = run_bass_kernel_spmd(nc, in_maps, core_ids=list(range(8)))
    return np.stack([r["y"] for r in res.results], axis=0)


# version 33
# speedup vs baseline: 1.0618x; 1.0001x over previous
import numpy as np
import ml_dtypes
import concourse.bass as bass
import concourse.mybir as mybir
from concourse.bass_utils import run_bass_kernel_spmd

F32 = mybir.dt.float32
BF16 = mybir.dt.bfloat16
I32 = mybir.dt.int32
AF = mybir.ActivationFunctionType
ALU = mybir.AluOpType
AX = mybir.AxisListType
bf16 = ml_dtypes.bfloat16

D = 1024
T = 2048
NH = 16
DH = 64
NC = 8
NT = 16
PI = float(np.pi)
EXP_C = 30.0

ENGS = ("pe", "act", "dve", "pool", "sync")
OWN = {"pe": "pe", "act": "act", "dve": "dve", "pool": "pool"}


class Sched:

    def __init__(self, nc):
        self.nc = nc
        self.q = {e: [] for e in ENGS}
        names = ["pe", "act", "dve", "pool", "dmas", "dmag", "out",
                 "x0", "x1", "x2", "x3"]
        self.sem = {k: nc.alloc_semaphore("s_" + k) for k in names}
        self.cnt = {k: 0 for k in names}
        self.watermark = {e: {k: 0 for k in names} for e in ENGS}

    def emit(self, eng, fn, waits=(), inc=None, amt=1):
        ws = []
        for s, v in waits:
            if v is None:
                continue
            assert v <= self.cnt[s], f"future token {s}={v} (now {self.cnt[s]})"
            if v > self.watermark[eng][s]:
                self.watermark[eng][s] = v
                ws.append((s, v))
        tok = None
        if inc is not None:
            self.cnt[inc] += amt
            tok = (inc, self.cnt[inc])
            if OWN.get(eng) == inc:
                self.watermark[eng][inc] = self.cnt[inc]
        sem = self.sem

        def run(e, fn=fn, ws=ws, inc=inc, amt=amt):
            for s, v in ws:
                e.wait_ge(sem[s], v)
            inst = fn(e)
            if inc is not None:
                inst.then_inc(sem[inc], amt)

        self.q[eng].append(run)
        return tok

    def tok(self, s):
        return (s, self.cnt[s])


def build(upto="full", taps=None):
    nc = bass.Bass("TRN2", target_bir_lowering=False)

    x = nc.dram_tensor("x", [T, D], F32, kind="ExternalInput")
    wqkv = nc.dram_tensor("wqkv", [D, 3 * D], F32, kind="ExternalInput")
    wo_d = nc.dram_tensor("wo", [D, D], F32, kind="ExternalInput")
    wft = nc.dram_tensor("wft", [D, D], F32, kind="ExternalInput")
    rsw_d = nc.dram_tensor("rsw", [128, 128], BF16, kind="ExternalInput")
    y = nc.dram_tensor("y", [T, D], F32, kind="ExternalOutput")

    xnT = nc.alloc_sbuf_tensor("xnT", [128, NC * T], BF16)
    qT = nc.alloc_sbuf_tensor("qT", [128, NC * T], BF16)
    kT = nc.alloc_sbuf_tensor("kT", [128, NC * T], BF16)
    vA = nc.alloc_sbuf_tensor("vA", [128, NT * NH * 65 + 64], BF16)
    wo8 = nc.alloc_sbuf_tensor("wo8", [128, 8192], BF16)
    swT = nc.alloc_sbuf_tensor("swT", [128, 8192], BF16)
    wd = nc.alloc_sbuf_tensor("wd", [128, 2048], BF16)
    qb = nc.alloc_sbuf_tensor("qb", [128, 1024], BF16)
    pt = nc.alloc_sbuf_tensor("pt", [128, 3072], BF16)
    rrow = nc.alloc_sbuf_tensor("rrow", [128, 2048], BF16)
    rrow32 = nc.alloc_sbuf_tensor("rrow32", [128, 512], F32)
    bcSB = nc.alloc_sbuf_tensor("bcSB", [64, 1024], BF16)
    ytt = nc.alloc_sbuf_tensor("ytt", [64, 1024], BF16)
    pool4 = nc.alloc_sbuf_tensor("pool4", [128, 4096], F32)
    poolb = nc.alloc_sbuf_tensor("poolb", [128, 4096], BF16)
    identf = nc.alloc_sbuf_tensor("identf", [128, 128], F32)
    rsw = nc.alloc_sbuf_tensor("rswb", [128, 128], BF16)
    onesb = nc.alloc_sbuf_tensor("onesb", [128, 64], BF16)
    biasC = nc.alloc_sbuf_tensor("biasC", [128, 4], F32)
    stats = nc.alloc_sbuf_tensor("stats", [128, 12], F32)
    mv = nc.alloc_sbuf_tensor("mv", [128, 2], F32)
    lncol = nc.alloc_sbuf_tensor("lncol", [128, 48], F32)
    ecol = nc.alloc_sbuf_tensor("ecol", [128, 1], F32)
    trow = swT.ap()[:, 0:4096].bitcast(F32)

    C4 = wo8[:, 0:2048]
    S4 = wo8[:, 2048:4096]

    def p4(i):
        return pool4[:, 1024 * i:1024 * (i + 1)]

    PSA = nc.alloc_psum_tensor("PSA", [128, 2048], F32)
    PSS = nc.alloc_psum_tensor("PSS", [128, 2048], F32)

    S = Sched(nc)
    env = {"xnT": xnT, "qT": qT, "kT": kT, "vA": vA, "wo8": wo8, "swT": swT,
           "pt": pt, "rrow": rrow, "pool4": pool4, "poolb": poolb, "trow": trow,
           "stats": stats, "mv": mv, "lncol": lncol, "qb": qb, "bcSB": bcSB,
           "ecol": ecol, "identf": identf, "onesb": onesb, "biasC": biasC}

    rsw_tok = S.emit("sync", lambda e: e.dma_start(out=rsw[:], in_=rsw_d.ap()),
                     inc="dmas", amt=16)

    S.emit("pool", lambda e: e.memset(identf[:], 1.0))
    id_tok = S.emit("pool", lambda e: e.affine_select(
        out=identf[:], in_=identf[:], pattern=[[-1, 128]],
        compare_op=ALU.is_equal, fill=0.0, channel_multiplier=1, base=0),
        inc="pool")
    ones_tok = S.emit("pool", lambda e: e.memset(onesb[:], 1.0), inc="pool")
    S.emit("pool", lambda e: e.memset(vA.ap()[:, NT * NH * 65:], 0.0))
    vones_tok = S.emit("pool", lambda e: e.memset(
        vA.ap()[:, 64:NT * NH * 65:65], 1.0), inc="pool")
    bias_tok = S.emit("dve", lambda e: e.memset(biasC[:, 0:1], -EXP_C), inc="dve")
    S.emit("dve", lambda e: e.memset(ecol[:], 1.0))
    S.emit("dve", lambda e: e.memset(ecol[0:32, :], -1.0))
    ecol_tok = S.emit("dve", lambda e: e.memset(ecol[64:96, :], -1.0), inc="dve")



    sqdump = xnT.ap()[:, 0:2048].bitcast(F32)
    red_toks = []
    xslot_tok = {}
    for tau in range(4):
        xslot_tok[tau % 4] = S.emit("sync", lambda e, tau=tau: e.dma_start(
            out=p4(tau), in_=x.ap()[128 * tau:128 * (tau + 1), :]),
            waits=[], inc="x%d" % (tau % 4), amt=16)
    redA = None
    for tau in range(NT):
        par = tau % 4
        xt = p4(par)
        S.emit("act", lambda e, xt=xt, tau=tau: e.activation(
            sqdump, xt, AF.Square, accum_out=lncol[:, 16 + tau:17 + tau]),
            waits=[xslot_tok[par]])
        redA = S.emit("dve", lambda e, xt=xt, tau=tau: e.reduce_sum(
            lncol[:, tau:tau + 1], xt, AX.X), waits=[xslot_tok[par], S.tok("act")],
            inc="dve")
        red_toks.append(redA)
        if tau + 4 < NT:
            xslot_tok[par] = S.emit("sync", lambda e, tau=tau, par=par: e.dma_start(
                out=p4(par), in_=x.ap()[128 * (tau + 4):128 * (tau + 5), :]),
                waits=[redA], inc="x%d" % par, amt=16)
    trow2 = swT.ap()[:, 4096:8192].bitcast(F32)
    it_tok = S.emit("pool", lambda e: e.iota(
        trow[0:1, :], [[1, 2048]], base=0, channel_multiplier=0,
        allow_small_or_imprecise_dtypes=True), inc="pool")
    S.emit("pool", lambda e: e.iota(
        trow2[0:1, 0:128].rearrange("a (r i) -> a r i", i=32),
        [[0, 4], [1, 32]], base=0, channel_multiplier=0,
        allow_small_or_imprecise_dtypes=True), inc="pool")
    S.emit("act", lambda e: e.activation(
        trow2[0:1, 0:128], trow2[0:1, 0:128], AF.Exp,
        scale=-float(np.log(10000.0)) / 32.0), waits=[S.tok("pool")])
    th_tok = S.emit("act", lambda e: e.drain(), inc="act")

    ang_tok = None
    for j in range(2):
        for s2 in range(2):
            cs = slice(1024 * j + 512 * s2, 1024 * j + 512 * s2 + 512)
            ang_tok = S.emit("pe", lambda e, cs=cs: e.matmul(
                PSA[:, cs], trow2[0:1, 0:128], trow[0:1, cs],
                start=True, stop=True), waits=[th_tok, it_tok], inc="pe")

    tgA = poolb.ap()[:, 0:2048].bitcast(F32)
    tgB = poolb.ap()[:, 2048:4096].bitcast(F32)
    sin_last = None
    for (dest, quarter, signed) in [(C4, True, False), (S4, False, True)]:
        for j in range(2):
            cs = slice(1024 * j, 1024 * (j + 1))
            w0 = [ang_tok] + ([sin_last] if sin_last else [])
            S.emit("dve", lambda e, cs=cs, quarter=quarter: e.tensor_scalar(
                tgA, PSA[:, cs], 1.0 / (2 * PI), 0.25 if quarter else 0.0,
                ALU.mult, ALU.add), waits=w0)
            S.emit("dve", lambda e: e.tensor_copy(tgB.bitcast(I32), tgA))
            S.emit("dve", lambda e: e.tensor_copy(tgA, tgB.bitcast(I32)))
            r_tok = S.emit("dve", lambda e, cs=cs: e.scalar_tensor_tensor(
                tgB, tgA, -2 * PI, PSA[:, cs], ALU.mult, ALU.add), inc="dve")
            if quarter:
                r_tok = S.emit("dve", lambda e: e.tensor_scalar_add(
                    tgB, tgB, PI / 2), inc="dve")
            if signed:
                s_tok = S.emit("act", lambda e: e.activation(
                    tgA, tgB, AF.Sin), waits=[r_tok], inc="act")
                sin_last = S.emit("dve", lambda e, dest=dest, cs=cs: e.tensor_scalar_mul(
                    dest[:, cs], tgA, ecol[:, 0:1]), waits=[s_tok, ecol_tok],
                    inc="dve")
            else:
                sin_last = S.emit("act", lambda e, dest=dest, cs=cs: e.activation(
                    dest[:, cs], tgB, AF.Sin), waits=[r_tok], inc="act")
    tables_tok = sin_last
    tables_dve = S.tok("dve")
    tables_act = S.tok("act")
    aA = S.emit("act", lambda e: e.drain(), inc="act")
    S.emit("dve", lambda e: e.drain(), waits=[aA])
    S.emit("dve", lambda e: e.tensor_scalar_mul(
        lncol[:, 0:16], lncol[:, 0:16], -1.0 / D))
    S.emit("dve", lambda e: e.tensor_scalar(
        lncol[:, 16:32], lncol[:, 16:32], 1.0 / D, 1e-5, ALU.mult, ALU.add))
    S.emit("dve", lambda e: e.drain())
    S.emit("dve", lambda e: e.tensor_tensor(
        lncol[:, 32:48], lncol[:, 0:16], lncol[:, 0:16], ALU.mult))
    S.emit("dve", lambda e: e.drain())
    S.emit("dve", lambda e: e.tensor_tensor(
        lncol[:, 16:32], lncol[:, 16:32], lncol[:, 32:48], ALU.subtract))
    vB = S.emit("dve", lambda e: e.drain(), inc="dve")
    S.emit("act", lambda e: e.activation(
        lncol[:, 32:48], lncol[:, 16:32], AF.Sqrt), waits=[vB])
    aB = S.emit("act", lambda e: e.drain(), inc="act")
    S.emit("dve", lambda e: e.reciprocal(lncol[:, 16:32], lncol[:, 32:48]),
           waits=[aB])
    S.emit("dve", lambda e: e.drain())
    S.emit("dve", lambda e: e.tensor_tensor(
        lncol[:, 32:48], lncol[:, 0:16], lncol[:, 16:32], ALU.mult))
    ln_tok = S.emit("dve", lambda e: e.drain(), inc="dve")

    evac_toks = [None, None]
    v_evac = []
    xnbufs = [pt.ap()[:, 0:2048].bitcast(F32),
              rrow.ap()[:, 0:2048].bitcast(F32)]
    wvfull = swT.ap()[:, 0:8192]
    wv_toks = [None, None]
    wv_toks[0] = S.emit("pool", lambda e: e.dma_start(
        out=wvfull.rearrange("p (c n) -> p c n", c=NC),
        in_=wqkv.ap()[:, 2048:3072].rearrange("(c p) n -> p c n", p=128)),
        waits=[tables_tok, tables_dve, tables_act], inc="dmag", amt=16)
    for tau in range(4):
        xslot_tok[tau % 4] = S.emit("sync", lambda e, tau=tau: e.dma_start(
            out=p4(tau), in_=x.ap()[128 * tau:128 * (tau + 1), :]),
            waits=[red_toks[min(tau + 12, NT - 1)]], inc="x%d" % (tau % 4),
            amt=16)
    tp_tok = None
    tp_prev = [None, None]
    vjobs = []
    for tau in range(NT):
        par = tau % 4
        xt = p4(par)
        xn = xnbufs[tau % 2]
        xw = [xslot_tok[par], ln_tok]
        if tp_prev[tau % 2] is not None:
            xw.append(tp_prev[tau % 2])
        xn_tok = S.emit("dve", lambda e, xt=xt, xn=xn, tau=tau: e.tensor_scalar(
            xn, xt, lncol[:, 16 + tau:17 + tau], lncol[:, 32 + tau:33 + tau],
            ALU.mult, ALU.add), waits=xw, inc="dve")
        if tau + 4 < NT:
            xslot_tok[par] = S.emit("sync", lambda e, tau=tau, par=par: e.dma_start(
                out=p4(par), in_=x.ap()[128 * (tau + 4):128 * (tau + 5), :]),
                waits=[xn_tok], inc="x%d" % par, amt=16)
        bpar = tau % 2
        pcols = 1024 * bpar
        for c in range(NC):
            w = [xn_tok, id_tok]
            if c == 0 and evac_toks[bpar] is not None:
                w.append(evac_toks[bpar])
            tp_tok = S.emit("pe", lambda e, xn=xn, c=c, pcols=pcols: e.transpose(
                PSS[:, pcols + 128 * c: pcols + 128 * (c + 1)],
                xn[:, 128 * c:128 * (c + 1)], identf[:]), waits=w,
                inc="pe" if c == NC - 1 else None) or tp_tok
        tp_prev[tau % 2] = tp_tok

        def mk_out(tau=tau):
            return xnT.ap().rearrange("p (c t) -> p c t", c=NC)[:, :, 128 * tau:128 * (tau + 1)]
        evac_toks[bpar] = S.emit("dve", lambda e, pcols=pcols, mo=mk_out: e.tensor_copy(
            mo(), PSS[:, pcols:pcols + 1024].rearrange("p (c r) -> p c r", c=NC)),
            waits=[tp_tok], inc="dve")

        def vjob(tau=tau, ev=evac_toks[bpar]):
            for vh in range(2):
                pcol = 512 * ((2 * tau + vh) % 4)
                mm_tok = None
                for c in range(NC):
                    w = []
                    if c == 0:
                        w = [S.tok("dmag"), ev]
                        if len(v_evac) >= 4:
                            w.append(v_evac[-4])
                    mm_tok = S.emit("pe", lambda e, c=c, tau=tau, pcol=pcol, vh=vh: e.matmul(
                        PSA[:, pcol:pcol + 512],
                        xnT.ap()[:, T * c + 128 * tau: T * c + 128 * (tau + 1)],
                        wvfull[:, 1024 * c + 512 * vh:1024 * c + 512 * vh + 512],
                        start=(c == 0), stop=(c == NC - 1)), waits=w,
                        inc="pe" if c == NC - 1 else None) or mm_tok

                def mk_vout(tau=tau, vh=vh):
                    a = vA.ap()[:, 0:NT * NH * 65].rearrange("p (t h s) -> p t h s", t=NT, h=NH)
                    return a[:, tau, 8 * vh:8 * vh + 8, 0:64]
                v_evac.append(S.emit("dve", lambda e, pcol=pcol, mo=mk_vout: e.tensor_copy(
                    mo(), PSA[:, pcol:pcol + 512].rearrange("p (h s) -> p h s", h=8)),
                    waits=[mm_tok, vones_tok], inc="dve"))
        vjobs.append(vjob)
        if tau >= 1:
            vjobs.pop(0)()
    while vjobs:
        vjobs.pop(0)()
    xnT_tok = evac_toks[(NT - 1) % 2]
    xnT_oth = evac_toks[NT % 2]

    if upto == "ph1":
        return finish(nc, S, y, taps, env)

    vA_tok = v_evac[-1]



    PR = PSS[:, 1024:2048]
    rope_toks = {}
    wd_tok = [None, None]
    wd_reader = [None, None]
    prep_state = {"pr_free": xnT_tok, "swap_live": None, "proj_last": None, "qb_ready": None}

    def make_prep(cc):
        dmas = []
        jobs = []
        for qk in range(2):
            kind = "q" if qk == 0 else "k"
            d = qk * 8 + cc
            buf = (2 * cc + qk) % 2
            dest = qT if qk == 0 else kT

            def dma_job(d=d, buf=buf):
                w = [wd_reader[buf]] if wd_reader[buf] is not None else []
                wd_tok[buf] = S.emit("pool", lambda e, d=d, buf=buf: e.dma_start(
                    out=wd[:, 1024 * buf:1024 * (buf + 1)]
                    .rearrange("p (c n) -> p c n", c=NC),
                    in_=wqkv.ap()[:, 128 * d:128 * (d + 1)]
                    .rearrange("(c p) n -> p c n", p=128)),
                    waits=w, inc="dmag", amt=16)
            dmas.append(dma_job)
            for j in range(2):
                for cg in range(4):
                    def proj_job(cg=cg, j=j, buf=buf):
                        for c in (2 * cg, 2 * cg + 1):
                            for s2 in range(2):
                                w = []
                                if c == 0 and s2 == 0:
                                    w = [S.tok("dmag"), xnT_tok, xnT_oth,
                                         prep_state["pr_free"]]
                                last = (c == NC - 1 and s2 == 1)
                                t = S.emit(
                                    "pe", lambda e, c=c, s2=s2, j=j, buf=buf: e.matmul(
                                        PR[:, 512 * s2:512 * (s2 + 1)],
                                        wd[:, 1024 * buf + 128 * c:
                                           1024 * buf + 128 * (c + 1)],
                                        xnT.ap()[:, T * c + 1024 * j + 512 * s2:
                                                 T * c + 1024 * j + 512 * (s2 + 1)],
                                        start=(c == 0), stop=(c == NC - 1)),
                                    waits=w, inc="pe" if last else None)
                                if last:
                                    wd_reader[buf] = t
                                    prep_state["proj_last"] = t
                    jobs.append(proj_job)

                def swap_job():
                    ev = S.emit("dve", lambda e: e.tensor_copy(
                        qb[:, 0:1024], PR[:, 0:1024]),
                        waits=[prep_state["proj_last"]], inc="dve")
                    prep_state["qb_ready"] = ev
                    S.emit("pe", lambda e: e.matmul(
                        PR[:, 0:512], rsw[:], qb[:, 0:512],
                        start=True, stop=True), waits=[ev, rsw_tok])
                    prep_state["swap_live"] = S.emit("pe", lambda e: e.matmul(
                        PR[:, 512:1024], rsw[:], qb[:, 512:1024],
                        start=True, stop=True), inc="pe")
                jobs.append(swap_job)

                def rope_job(j=j, cc=cc, kind=kind, dest=dest):
                    sw2 = prep_state["swap_live"]
                    a_tok = S.emit("pool", lambda e, j=j: e.tensor_tensor(
                        poolb[:, 0:1024], qb[:, 0:1024],
                        C4[:, 1024 * j:1024 * (j + 1)], ALU.mult),
                        waits=[tables_tok, tables_dve, tables_act, vA_tok,
                               prep_state["qb_ready"]], inc="pool")
                    b_tok = S.emit("dve", lambda e, j=j: e.tensor_tensor(
                        poolb[:, 1024:2048], PR[:, 0:1024],
                        S4[:, 1024 * j:1024 * (j + 1)], ALU.mult),
                        waits=[sw2], inc="dve")
                    rt = S.emit("pool", lambda e, dest=dest, cc=cc, j=j:
                                e.tensor_tensor(
                                    dest[:, T * cc + 1024 * j:
                                         T * cc + 1024 * (j + 1)],
                                    poolb[:, 0:1024], poolb[:, 1024:2048],
                                    ALU.add), waits=[a_tok, b_tok], inc="pool")
                    rope_toks[(kind, cc, j)] = rt
                    prep_state["pr_free"] = b_tok
                jobs.append(rope_job)
        if cc == NC - 1:
            def wo_job():
                wo_holder["tok"] = S.emit("pool", lambda e: e.dma_start(
                    out=wo8.ap().rearrange("p (c n) -> p c n", c=NC),
                    in_=wo_d.ap().rearrange("(c p) n -> p c n", p=128)),
                    waits=[rope_toks[("k", NC - 1, 1)]], inc="dmag", amt=16)
            jobs.append(wo_job)
        return dmas, jobs

    def quarter_pieces():
        out = []
        for Q in range(4):
            for kt in range(min(16, 4 * Q + 4)):
                qs = max(512 * Q, 128 * kt)
                out.append((Q, kt, qs, 512 * (Q + 1)))
        return out

    PIECES = quarter_pieces()
    NP = len(PIECES)

    exp_tok = {}
    mask_tok = {}
    pv_tok = {}
    norm_tok = {}
    recip_tok = {}
    bcast_tok = {}
    for hh in range(2):
        for par in range(2):
            norm_tok[(hh, par)] = vA_tok
    pc_holder = [0]
    pending = []

    def pop_one():
        pending.pop(0)[2]()

    def run_chunk(cc, prep_dmas, prep_jobs):
        h_e, h_o = 2 * cc, 2 * cc + 1
        kw = [rope_toks[("k", cc, 0)], rope_toks[("k", cc, 1)],
              rope_toks[("q", cc, 0)], rope_toks[("q", cc, 1)], vA_tok]
        base = pc_holder[0]
        last_pv = {h_e: None, h_o: None}

        for i, (Q, kt, qs, qe) in enumerate(PIECES):
            p = base + i
            ln = qe - qs
            slot3 = p % 3
            par = Q % 2
            while (p - 3) >= 0 and (p - 3) not in pv_tok:
                assert pending, "pipeline stuck"
                pop_one()
            for hh in range(2):
                rb = 64 * hh
                w = list(kw)
                if (p - 1, hh) in exp_tok:
                    w.append(exp_tok[(p - 1, hh)])
                sm = S.emit("pe", lambda e, kt=kt, qs=qs, qe=qe, cc=cc, rb=rb, hh=hh: e.matmul(
                    PSS[:, 512 * hh:512 * hh + (qe - qs)],
                    kT.ap()[rb:rb + 64, T * cc + 128 * kt: T * cc + 128 * (kt + 1)],
                    qT.ap()[rb:rb + 64, T * cc + qs: T * cc + qe],
                    start=True, stop=True, tile_position=(rb, 0)),
                    waits=w, inc="pe")
                ew = [sm, bias_tok]
                if (p - 3, hh) in pv_tok:
                    ew.append(pv_tok[(p - 3, hh)])
                exp_tok[(p, hh)] = S.emit("act", lambda e, slot3=slot3, ln=ln, hh=hh: e.activation(
                    pt[:, 1024 * slot3 + 512 * hh: 1024 * slot3 + 512 * hh + ln],
                    PSS[:, 512 * hh:512 * hh + ln],
                    AF.Exp, bias=biasC[:, 0:1], scale=0.125), waits=ew, inc="act")
                if qs == 128 * kt:
                    mask_tok[(p, hh)] = S.emit("pool", lambda e, slot3=slot3, hh=hh: e.affine_select(
                        out=pt[:, 1024 * slot3 + 512 * hh:
                               1024 * slot3 + 512 * hh + 128],
                        in_=pt[:, 1024 * slot3 + 512 * hh:
                               1024 * slot3 + 512 * hh + 128],
                        pattern=[[1, 128]], compare_op=ALU.is_ge, fill=0.0,
                        channel_multiplier=-1, base=0),
                        waits=[exp_tok[(p, hh)]], inc="pool")

            def pv_job(p=p, Q=Q, kt=kt, qs=qs, qe=qe, slot3=slot3, par=par):
                stop = (kt == min(15, 4 * Q + 3))
                tok = None
                for hh, h in ((0, h_e), (1, h_o)):
                    w = [exp_tok[(p, hh)]]
                    if (p, hh) in mask_tok:
                        w.append(mask_tok[(p, hh)])
                    if kt == 0:
                        w.append(norm_tok[(hh, par)])
                    ycol = 1024 * hh + 512 * par
                    tok = S.emit("pe", lambda e, h=h, hh=hh, kt=kt, qs=qs, qe=qe,
                                 slot3=slot3, Q=Q, stop=stop, ycol=ycol: e.matmul(
                        PSA[0:128, ycol + qs - 512 * Q: ycol + qe - 512 * Q],
                        vA.ap()[:, (kt * NH + h) * 65:(kt * NH + h) * 65 + 128],
                        pt[:, 1024 * slot3 + 512 * hh:
                           1024 * slot3 + 512 * hh + (qe - qs)],
                        start=(kt == 0), stop=stop, skip_group_check=True),
                        waits=w, inc="pe")
                    last_pv[h] = tok
                    pv_tok[(p, hh)] = tok
                pv_tok[p] = tok
            pending.append(("pv", p, pv_job))


            if kt == min(15, 4 * Q + 3):
                for hh, h in ((0, h_e), (1, h_o)):
                    def recipB_job(hh=hh, h=h, par=par):
                        def _r(e):
                            with nc.allow_low_precision(
                                    reason="recip row feeds bf16 bcast"):
                                return e.reciprocal(
                                    rrow[64:65, 512 * (2 * hh + par):
                                         512 * (2 * hh + par) + 512],
                                    PSA[64:65, 1024 * hh + 512 * par:
                                        1024 * hh + 512 * par + 512])
                        recip_tok[(hh, par, 1)] = S.emit(
                            "dve", _r, waits=[last_pv[h]], inc="dve")
                    pending.append(("dve", None, recipB_job))
                for hh, h in ((0, h_e), (1, h_o)):
                    def bcastB_job(hh=hh, par=par):
                        bcast_tok[(hh, par, 1)] = S.emit(
                            "pe", lambda e, hh=hh, par=par: e.matmul(
                                PSA[64:128, 1024 * hh + 512 * par:
                                    1024 * hh + 512 * par + 512],
                                onesb[64:65, 0:64],
                                rrow[64:65, 512 * (2 * hh + par):
                                     512 * (2 * hh + par) + 512],
                                start=True, stop=True, skip_group_check=True),
                            waits=[recip_tok[(hh, par, 1)], ones_tok],
                            inc="pe")
                    pending.append(("pe", p + 7 + hh, bcastB_job))

                    def norm_job(hh=hh, Q=Q, cc=cc, par=par):
                        evb = S.emit("dve", lambda e, hh=hh, par=par: e.tensor_copy(
                            bcSB[:, 512 * hh:512 * (hh + 1)],
                            PSA[64:128, 1024 * hh + 512 * par:
                                1024 * hh + 512 * par + 512]),
                            waits=[bcast_tok[(hh, par, 1)]], inc="dve")
                        cs = slice(T * cc + 512 * Q, T * cc + 512 * (Q + 1))
                        if hh == 0:
                            nt = S.emit("dve", lambda e, hh=hh, par=par, cs=cs:
                                        e.tensor_tensor(
                                qT.ap()[0:64, cs],
                                PSA[0:64, 1024 * hh + 512 * par:
                                    1024 * hh + 512 * par + 512],
                                bcSB[:, 512 * hh:512 * (hh + 1)],
                                ALU.mult), waits=[evb], inc="dve")
                        else:
                            t1 = S.emit("dve", lambda e, hh=hh, par=par: e.tensor_tensor(
                                ytt[:, 0:512],
                                PSA[0:64, 1024 * hh + 512 * par:
                                    1024 * hh + 512 * par + 512],
                                bcSB[:, 512 * hh:512 * (hh + 1)],
                                ALU.mult), waits=[evb], inc="dve")
                            nt = S.emit("dve", lambda e, cs=cs: e.tensor_copy(
                                qT.ap()[64:128, cs], ytt[:, 0:512]),
                                waits=[t1], inc="dve")
                        norm_tok[(hh, par)] = nt
                    pending.append(("dve", None, norm_job))

            popped_pe = False
            while pending:
                tag, pp, _ = pending[0]
                if tag == "dve":
                    pop_one()
                elif tag == "pv" and pp <= p - 1:
                    pop_one()
                elif tag == "pe" and not popped_pe and (pp is None or pp <= p):
                    pop_one()
                    popped_pe = True
                else:
                    break
            if i < len(prep_dmas):
                prep_dmas[i]()
            elif i >= 6 and prep_jobs:
                prep_jobs.pop(0)()

        while prep_jobs:
            prep_jobs.pop(0)()
        pc_holder[0] = base + NP

    d0, j0 = make_prep(0)
    for f in d0 + j0:
        f()

    wo_holder = {}
    swh = {}

    def emit_sw():
        sw_toks = []
        for c in range(NC):
            par = c % 2
            ddw = [tables_tok, tables_act, tables_dve, S.tok("dve"), S.tok("pe")]
            if c >= 2:
                ddw.append(sw_toks[c - 2])
            S.emit("sync", lambda e, c=c, par=par: e.dma_start(
                out=p4(par), in_=wft.ap()[128 * c:128 * (c + 1), :]),
                waits=ddw, inc="dmas", amt=16)
            sw_toks.append(S.emit("act", lambda e, c=c, par=par: e.sign(
                swT[:, 1024 * c:1024 * (c + 1)], p4(par)), waits=[S.tok("dmas")],
                inc="act"))
        swh["tok"] = sw_toks[-1]

    PRE = 3
    gpre = [qb[:, 0:1024], wd[:, 0:1024], wd[:, 1024:2048]]
    gpre_tok = [None] * PRE
    gpre_mm = [None] * PRE
    pre_jobs = []

    def mk_pre(tau):
        def grp(cg, tau=tau):
            for c in (2 * cg, 2 * cg + 1):
                for s2 in range(2):
                    w = []
                    if c == 0 and s2 == 0:
                        w = [swh["tok"], rope_toks[("k", NC - 1, 1)]]
                        if tau > 0:
                            w.append(gpre_tok[tau - 1])
                    last = (c == NC - 1 and s2 == 1)
                    t = S.emit("pe", lambda e, c=c, s2=s2, tau=tau: e.matmul(
                        PR[:, 512 * s2:512 * (s2 + 1)],
                        xnT.ap()[:, T * c + 128 * tau:T * c + 128 * (tau + 1)],
                        swT[:, 1024 * c + 512 * s2:1024 * c + 512 * (s2 + 1)],
                        start=(c == 0), stop=(c == NC - 1)),
                        waits=w, inc="pe" if last else None)
                    if last:
                        gpre_mm[tau] = t
        for cg in range(4):
            pre_jobs.append(lambda cg=cg, tau=tau: grp(cg, tau))

        def silu_job(tau=tau):
            gpre_tok[tau] = S.emit("act", lambda e, tau=tau: e.activation(
                gpre[tau], PR[:, 0:1024], AF.Silu), waits=[gpre_mm[tau]],
                inc="act")
        pre_jobs.append(silu_job)

    for tau in range(PRE):
        mk_pre(tau)

    for cc in range(NC):
        nd, nj = make_prep(cc + 1) if cc + 1 < NC else ([], pre_jobs)
        run_chunk(cc, nd, nj)
        if cc == 0:
            emit_sw()
    while pending:
        pop_one()
    yT_tok = norm_tok[(1, 1)]
    yT_all = [norm_tok[k] for k in norm_tok]
    wo_tok = wo_holder["tok"]

    if upto == "ph3":
        return finish(nc, S, y, taps, env)

    regs = [PSA[:, 0:1024], PSA[:, 1024:2048], PSS[:, 0:1024], PSS[:, 1024:2048]]
    reg_free = [None, None, None, None]
    ph3_guard = yT_all + [S.tok("dve"), S.tok("act"), S.tok("pool")]
    swT_tok = swh["tok"]
    wo_dmag = S.tok("dmag")
    xrv = [poolb.ap()[:, 0:2048].bitcast(F32),
           poolb.ap()[:, 2048:4096].bitcast(F32)]
    ostg = [p4(3), p4(1)]
    xr_toks = []
    for i in range(2):
        xr_toks.append(S.emit("pool", lambda e, i=i: e.dma_start(
            out=xrv[i], in_=x.ap()[128 * i:128 * (i + 1), :]),
            waits=(ph3_guard if i == 0 else []), inc="dmag", amt=16))
    g_free = None
    out_free = [None, None]
    for tau in range(NT):
        fr = (2 * tau) % 4
        wr = (2 * tau + 1) % 4
        ffn_tok = None
        if tau >= PRE:
            for c in range(NC):
                for s2 in range(2):
                    w = []
                    if c == 0 and s2 == 0:
                        w = [swT_tok] + ph3_guard
                        if reg_free[fr] is not None:
                            w.append(reg_free[fr])
                    ffn_tok = S.emit("pe", lambda e, c=c, s2=s2, fr=fr, tau=tau: e.matmul(
                        regs[fr][:, 512 * s2:512 * (s2 + 1)],
                        xnT.ap()[:, T * c + 128 * tau: T * c + 128 * (tau + 1)],
                        swT[:, 1024 * c + 512 * s2: 1024 * c + 512 * (s2 + 1)],
                        start=(c == 0), stop=(c == NC - 1)), waits=w,
                        inc="pe" if (c == NC - 1 and s2 == 1) else None) or ffn_tok
        wo_mm = None
        for c in range(NC):
            for s2 in range(2):
                w = []
                if c == 0 and s2 == 0:
                    w = [wo_dmag, yT_tok]
                    if reg_free[wr] is not None:
                        w.append(reg_free[wr])
                wo_mm = S.emit("pe", lambda e, c=c, s2=s2, wr=wr, tau=tau: e.matmul(
                    regs[wr][:, 512 * s2:512 * (s2 + 1)],
                    qT.ap()[:, T * c + 128 * tau: T * c + 128 * (tau + 1)],
                    wo8[:, 1024 * c + 512 * s2: 1024 * c + 512 * (s2 + 1)],
                    start=(c == 0), stop=(c == NC - 1)), waits=w,
                    inc="pe" if (c == NC - 1 and s2 == 1) else None) or wo_mm
        if tau < PRE:
            t_tok = S.emit("dve", lambda e, wr=wr, tau=tau: e.tensor_tensor(
                p4(2), regs[wr][:], gpre[tau], ALU.mult),
                waits=[wo_mm, gpre_tok[tau]], inc="dve")
            reg_free[wr] = t_tok
        else:
            gw = [ffn_tok]
            if g_free is not None:
                gw.append(g_free)
            g_tok = S.emit("act", lambda e, fr=fr: e.activation(
                p4(0), regs[fr][:], AF.Silu), waits=gw, inc="act")
            reg_free[fr] = g_tok
            t_tok = S.emit("dve", lambda e, wr=wr: e.tensor_tensor(
                p4(2), regs[wr][:], p4(0), ALU.mult), waits=[wo_mm, g_tok],
                inc="dve")
            reg_free[wr] = t_tok
            g_free = t_tok
        ow = [t_tok, xr_toks[tau]]
        if out_free[tau % 2] is not None:
            ow.append(out_free[tau % 2])
        o_tok = S.emit("dve", lambda e, tau=tau: e.tensor_tensor(
            ostg[tau % 2], p4(2), xrv[tau % 2], ALU.add), waits=ow,
            inc="dve")
        if tau + 2 < NT:
            xr_toks.append(S.emit("pool", lambda e, tau=tau: e.dma_start(
                out=xrv[tau % 2],
                in_=x.ap()[128 * (tau + 2):128 * (tau + 3), :]),
                waits=[o_tok], inc="dmag", amt=16))
        out_free[tau % 2] = S.emit("sync", lambda e, tau=tau: e.dma_start(
            out=y.ap()[128 * tau:128 * (tau + 1), :], in_=ostg[tau % 2]),
            waits=[o_tok], inc="out", amt=16)

    return finish(nc, S, y, taps, env)


def finish(nc, S, y, taps, env):
    if taps:
        for (name, shape, dt, get_ap) in taps:
            od = nc.dram_tensor(name, shape, dt, kind="ExternalOutput")
            waits = [(k, S.cnt[k]) for k in ("pe", "act", "dve", "pool")]
            S.emit("sync", lambda e, od=od, get_ap=get_ap: e.dma_start(
                out=od.ap(), in_=get_ap(env)), waits=waits, inc="out", amt=16)
    final_waits = [(k, S.cnt[k]) for k in S.cnt if S.cnt[k] > 0]
    sem = S.sem

    def fin(e):
        for s, v in final_waits:
            e.wait_ge(sem[s], v)
    S.q["sync"].append(fin)

    with nc.Block() as block:
        @block.sync
        def _(e):
            for f in S.q["sync"]:
                f(e)

        @block.tensor
        def _(e):
            for f in S.q["pe"]:
                f(e)

        @block.scalar
        def _(e):
            for f in S.q["act"]:
                f(e)

        @block.vector
        def _(e):
            for f in S.q["dve"]:
                f(e)

        @block.gpsimd
        def _(e):
            for f in S.q["pool"]:
                f(e)
    return nc


def make_rsw():
    r = np.zeros((128, 128), dtype=np.float32)
    for m in range(128):
        r[(m + 32) if m % 64 < 32 else (m - 32), m] = 1.0
    return r.astype(bf16)


def kernel(L0_attn_x, L0_attn_W_qkv, L0_attn_W_o, L0_ffn_weight):
    x = np.ascontiguousarray(np.asarray(L0_attn_x, dtype=np.float32))
    wqkv = np.ascontiguousarray(np.asarray(L0_attn_W_qkv, dtype=np.float32))
    wo = np.ascontiguousarray(np.asarray(L0_attn_W_o, dtype=np.float32))
    wft = np.ascontiguousarray(np.asarray(L0_ffn_weight, dtype=np.float32).T)
    rsw = make_rsw()

    nc = build("full")
    in_maps = [{"x": np.ascontiguousarray(x[c]), "wqkv": wqkv, "wo": wo,
                "wft": wft, "rsw": rsw} for c in range(8)]
    res = run_bass_kernel_spmd(nc, in_maps, core_ids=list(range(8)))
    return np.stack([r["y"] for r in res.results], axis=0)
